# version 01
# baseline (speedup 1.0000x reference)
import numpy as np
import ml_dtypes
import concourse.bass as bass
import concourse.mybir as mybir

F32 = mybir.dt.float32
BF16 = mybir.dt.bfloat16
AX = mybir.AxisListType.X
ALU = mybir.AluOpType
ACT = mybir.ActivationFunctionType

MAGIC = float(np.float32(1.5 * 2 ** 23))
RSQ = float(np.float32(1.0) / np.sqrt(np.float32(128.0)))
D = 2048
DC = 16
TPC = 1024
HID = 8192
OC = 64
T = 8192
WGV_N = 16384 * 2048
WOUT_N = 2048 * 8192


class B:
    def __init__(self, nc):
        self.nc = nc
        self.sem = {}
        self.cnt = {}

    def mksem(self, *names):
        for n in names:
            self.sem[n] = self.nc.semaphore(n).__enter__()
            self.cnt[n] = 0

    def inc(self, instr, name, k=1):
        instr.then_inc(self.sem[name], k)
        self.cnt[name] += k
        return instr

    def w(self, eng, name, n=None):
        eng.wait_ge(self.sem[name], self.cnt[name] if n is None else n)

    def bar(self):
        self.nc.all_engine_barrier()


def v3(ap2d, n, m):
    return ap2d.rearrange("p (a c) -> p a c", a=n)


def build(stage=3, sub=0):
    nc = bass.Bass("TRN2", target_bir_lowering=False)
    b = B(nc)

    x_c = nc.dram_tensor("x_c", [TPC, D], F32, kind="ExternalInput").ap()
    wgv_c = nc.dram_tensor("wgv_c", [2048, 2048], F32, kind="ExternalInput").ap()
    wout_c = nc.dram_tensor("wout_c", [256, 8192], F32, kind="ExternalInput").ap()
    nw_in = nc.dram_tensor("nw_in", [1, D], F32, kind="ExternalInput").ap()
    wq_c = nc.dram_tensor("wq_c", [D, 256], F32, kind="ExternalInput").ap()
    wk_c = nc.dram_tensor("wk_c", [D, 256], F32, kind="ExternalInput").ap()
    wv_c = nc.dram_tensor("wv_c", [D, 256], F32, kind="ExternalInput").ap()
    wo_in = nc.dram_tensor("wo_in", [D, D], F32, kind="ExternalInput").ap()
    bq_c = nc.dram_tensor("bq_c", [1, 256], F32, kind="ExternalInput").ap()
    bk_c = nc.dram_tensor("bk_c", [1, 256], F32, kind="ExternalInput").ap()
    bv_c = nc.dram_tensor("bv_c", [1, 256], F32, kind="ExternalInput").ap()
    bo_in = nc.dram_tensor("bo_in", [1, D], F32, kind="ExternalInput").ap()
    y = nc.dram_tensor("y", [TPC, D], F32, kind="ExternalOutput").ap()

    u_hbm = nc.dram_tensor("u_hbm", [OC, 128, TPC], F32, kind="Internal").ap()
    k2_hbm = nc.dram_tensor("k2_hbm", [OC, 128, TPC], BF16, kind="Internal").ap()
    xsp = nc.dram_tensor("xsp", [128, DC * TPC], F32, kind="Internal").ap()
    ag_wgv_in = nc.dram_tensor("ag_wgv_in", [2048, 2048], BF16, kind="Internal").ap()
    ag_wgv = nc.dram_tensor("ag_wgv", [8 * 2048, 2048], BF16, kind="Internal",
                            addr_space="Shared").ap()
    ag_wout_in = nc.dram_tensor("ag_wout_in", [8192, 256], BF16,
                                kind="Internal").ap()
    ag_wout = nc.dram_tensor("ag_wout", [8 * 8192, 256], BF16, kind="Internal",
                             addr_space="Shared").ap()
    ar_s_in = nc.dram_tensor("ar_s_in", [1, 2], F32, kind="Internal").ap()
    ar_s = nc.dram_tensor("ar_s", [1, 2], F32, kind="Internal").ap()
    ag_ln_in = nc.dram_tensor("ag_ln_in", [2048, TPC], BF16, kind="Internal").ap()
    ag_ln = nc.dram_tensor("ag_ln", [8 * 2048, TPC], BF16, kind="Internal",
                           addr_space="Shared").ap()
    ag_o_in = nc.dram_tensor("ag_o_in", [256, T], BF16, kind="Internal").ap()
    ag_o = nc.dram_tensor("ag_o", [8 * 256, T], BF16, kind="Internal",
                          addr_space="Shared").ap()

    h_idf = nc.inline_tensor(np.eye(128, dtype=np.float32), name="h_idf").ap()
    h_idb = nc.inline_tensor(np.eye(128, dtype=ml_dtypes.bfloat16),
                             name="h_idb").ap()
    h_ocf = nc.inline_tensor(np.ones((128, 1), np.float32), name="h_ocf").ap()
    h_ocb = nc.inline_tensor(np.ones((128, 1), ml_dtypes.bfloat16),
                             name="h_ocb").ap()
    h_orf = nc.inline_tensor(np.ones((1, 128), np.float32), name="h_orf").ap()

    b.mksem("ld", "g", "mm", "v", "sc", "cc", "st", "slab", "free", "ps", "ev",
            "x1", "x2", "aux", "ug", "k2s", "tq", "stA", "stB", "ugA", "ugB",
            "k2A", "k2B")

    from contextlib import ExitStack
    with ExitStack() as es:
        S = lambda name, shape, dt: es.enter_context(nc.sbuf_tensor(name, shape, dt))
        idf = S("idf", [128, 128], F32)
        idb = S("idb", [128, 128], BF16)
        ocf = S("c_ocf", [128, 1], F32)
        ocb = S("c_ocb", [128, 1], BF16)
        orf = S("c_orf", [1, 128], F32)
        nw_pp = S("nw_pp", [128, DC], F32)
        bo_pp = S("bo_pp", [128, DC], F32)
        bq_pp = S("bq_pp", [128, 8], F32)
        bk_pp = S("bk_pp", [128, 8], F32)
        bqs_pp = S("bqs_pp", [128, 8], F32)
        bqs_cb = S("bqs_cb", [128, 8], BF16)
        bk_cb = S("bk_cb", [128, 8], BF16)
        bv_row = S("bv_row", [1, 256], F32)
        bvb = S("bvb", [128, 256], F32)
        ceps = S("ceps", [1, 1], F32)
        rowA = S("rowA", [1, TPC], F32)
        rowB = S("rowB", [1, TPC], F32)
        rowC = S("rowC", [1, TPC], F32)
        rowD = S("rowD", [1, TPC], F32)
        srow = S("srow", [1, 16], F32)
        scol = S("scol", [128, 16], F32)
        mbt = S("mbt", [128, TPC], F32)
        rbt = S("rbt", [128, TPC], F32)
        fbt = S("fbt", [128, TPC], F32)
        cbt = S("cbt", [128, TPC], F32)
        dmask = S("dmask", [128, 4 * 512], F32)
        gmc = S("gmc", [128, 8], F32)
        zerot = S("zerot", [128, 128], BF16)
        b.ten = dict(idf=idf, idb=idb, ocf=ocf, ocb=ocb, orf=orf, nw_pp=nw_pp,
                     bo_pp=bo_pp, bq_pp=bq_pp, bk_pp=bk_pp, bqs_pp=bqs_pp,
                     bqs_cb=bqs_cb, bk_cb=bk_cb, bv_row=bv_row, bvb=bvb,
                     ceps=ceps, rowA=rowA, rowB=rowB, rowC=rowC, rowD=rowD,
                     srow=srow, scol=scol, mbt=mbt, rbt=rbt, fbt=fbt, cbt=cbt,
                     dmask=dmask, gmc=gmc, zerot=zerot)
        b.hbm = dict(u_hbm=u_hbm, k2_hbm=k2_hbm, xsp=xsp, ag_wgv=ag_wgv,
                     ag_wout=ag_wout, ag_ln_in=ag_ln_in, ag_ln=ag_ln,
                     ag_o_in=ag_o_in, ag_o=ag_o)

        with nc.Block() as _blk:
            sy, gp, te, ve, sc = nc.sync, nc.gpsimd, nc.tensor, nc.vector, nc.scalar
            for dst, src in ((idf, h_idf), (idb, h_idb), (ocf, h_ocf),
                             (ocb, h_ocb), (orf, h_orf), (bv_row, bv_c)):
                b.inc(sy.dma_start(out=dst[:, :], in_=src[:, :]), "ld", 16)
            with nc.allow_non_contiguous_dma(reason="tiny per-partition vecs"):
                for dst, src in ((nw_pp, nw_in), (bo_pp, bo_in)):
                    b.inc(sy.dma_start(
                        out=dst[:, :],
                        in_=src.rearrange("o (dc p) -> (o p) dc", p=128)),
                        "ld", 16)
                for dst, src in ((bq_pp, bq_c), (bk_pp, bk_c)):
                    b.inc(sy.dma_start(
                        out=dst[:, 0:2],
                        in_=src.rearrange("o (h p) -> (o p) h", p=128)),
                        "ld", 16)
            b.w(ve, "ld")
            ve.memset(zerot[:, :], 0.0)
            ve.memset(ceps[:, :], 1e-5)
            b.inc(ve.tensor_scalar_mul(bqs_pp[:, 0:8], bq_pp[:, 0:8], RSQ),
                  "v")
            ve.wait_ge(b.sem["v"], b.cnt["v"])
            ve.tensor_copy(bqs_cb[:, 0:8], bqs_pp[:, 0:8])
            b.inc(ve.tensor_copy(bk_cb[:, 0:8], bk_pp[:, 0:8]), "v")
            gp.memset(dmask[:, :], 0.0)
            for s_ in range(4):
                gp.affine_select(
                    out=dmask[:, s_ * 512:(s_ + 1) * 512],
                    in_=dmask[:, s_ * 512:(s_ + 1) * 512],
                    pattern=[[1, 512]], compare_op=ALU.is_ge, fill=-1e10,
                    base=-128 * s_, channel_multiplier=-1)
            b.inc(gp.engine_nop(), "g")

            b.dump_srow = (sub == 24)
            b.dump_y = y
            emit_prepass(b, wgv_c, wout_c, ag_wgv_in, ag_wgv, ag_wout_in,
                         ag_wout, ar_s_in, ar_s)
            if sub == 24:
                b.w(gp, "stA")
                return nc

            if sub == 25:
                b.w(gp, "cc", 3)
                b.inc(gp.dma_start(out=y[:, :], in_=ag_wgv[0:1024, :]),
                      "stA", 16)
                b.w(gp, "stA")
                return nc
            if sub == 26:
                b.w(gp, "cc", 3)
                b.inc(gp.dma_start(out=y.rearrange("a (c d) -> (a c) d", c=8),
                                   in_=ag_wout[0:8192, :]), "stA", 16)
                b.w(gp, "stA")
                return nc
            b.y_dbg = y
            with nc.sbuf_tensor("xT", [128, DC * TPC], F32) as xT:
                b.bar()
                emit_xload(b, x_c, xT)
                b.bar()
                if sub == 9:
                    emit_store_nat(b, xT, y)
                    b.w(gp, "stA")
                    b.w(gp, "stB")
                    return nc
                emit_ffn(b, xT, tag="f1", sub=sub)
                b.bar()
                if sub in (31, 32, 33, 35):
                    return nc
                if sub in (41, 42) and sub == 42:
                    emit_store_nat(b, xT, y)
                    b.w(nc.gpsimd, "stA")
                    b.w(nc.gpsimd, "stB")
                    return nc
                if sub == 41:
                    emit_store_nat(b, xT, y)
                    b.w(nc.gpsimd, "stA")
                    b.w(nc.gpsimd, "stB")
                    return nc
                if stage == 1:
                    emit_store_nat(b, xT, y)
                    b.w(gp, "stA")
                    b.w(gp, "stB")
                    return nc
                emit_ln_stage(b, xT)
                b.bar()
            if sub == 50:
                with nc.sbuf_tensor("xTr", [128, DC * TPC], F32) as xTr:
                    b.w(sy, "cc")
                    b.inc(sy.dma_start(out=xTr[:, :],
                                       in_=b.hbm["xsp"][:, :]), "x2", 16)
                    b.w(te, "x2")
                    b.bar()
                    emit_store_nat(b, xTr, y)
                    b.w(gp, "stA")
                    b.w(gp, "stB")
                return nc
            emit_attention(b, wq_c, wk_c, wv_c, sub=sub)
            b.bar()
            if sub in (53, 54):
                return nc
            if sub in (51, 52):
                with nc.sbuf_tensor("xTr", [128, DC * TPC], F32) as xTr:
                    b.inc(sy.dma_start(out=xTr[:, :],
                                       in_=b.hbm["xsp"][:, :]), "x2", 16)
                    b.w(te, "x2")
                    b.bar()
                    emit_store_nat(b, xTr, y)
                    b.w(gp, "stA")
                    b.w(gp, "stB")
                return nc
            with nc.sbuf_tensor("xT2", [128, DC * TPC], F32) as xT2:
                emit_outproj(b, wo_in, xT2)
                b.bar()
                if stage == 2:
                    emit_store_nat(b, xT2, y)
                    b.w(gp, "stA")
                    b.w(gp, "stB")
                    return nc
                emit_ffn(b, xT2, tag="f2")
                b.bar()
                emit_store_nat(b, xT2, y)
                b.w(gp, "stA")
                b.w(gp, "stB")
    return nc


def emit_prepass(b, wgv_c, wout_c, ag_wgv_in, ag_wgv, ag_wout_in, ag_wout,
                 ar_s_in, ar_s):
    nc = b.nc
    sy, gp, te, ve, sc = nc.sync, nc.gpsimd, nc.tensor, nc.vector, nc.scalar
    t_ = b.ten
    srow, scol, ocf, orf, idb = (t_["srow"], t_["scol"], t_["ocf"], t_["orf"],
                                 t_["idb"])
    rowA, rowB = t_["rowA"], t_["rowB"]
    with (
        nc.sbuf_tensor("pwA", [128, 2048], F32) as pwA,
        nc.sbuf_tensor("pwB", [128, 2048], F32) as pwB,
        nc.sbuf_tensor("pw_red24", [128, 32], F32) as pw_red24,
        nc.sbuf_tensor("pw_q", [128, 2048], BF16) as pw_q,
        nc.sbuf_tensor("pw_qT", [128, 2048], BF16) as pw_qT,
        nc.psum_tensor("pp_sum", [1, 32], F32) as pp_sum,
        nc.psum_tensor("pp_t", [128, 512], BF16) as pp_t,
        nc.psum_tensor("pp_c", [128, 8], F32) as pp_c,
        nc.psum_tensor("pp_sr", [1, 128], F32) as pp_sr,
    ):
        bufs = [pwA, pwB]
        b.w(te, "ld")
        ve.memset(pw_red24[:, :], 0.0)
        base_x1 = b.cnt["x1"]
        base_free = b.cnt["free"]
        for i in range(24):
            buf = bufs[i % 2]
            if i >= 2:
                b.w(sy, "free", base_free + i - 1)
            if i < 16:
                src = wgv_c[i * 128:(i + 1) * 128, :]
            else:
                wb = i - 16
                pb, og = wb // 4, wb % 4
                src = wout_c[pb * 128:(pb + 1) * 128,
                             og * 2048:(og + 1) * 2048]
            b.inc(sy.dma_start(out=buf[:, :], in_=src), "x1", 16)
            b.w(ve, "x1", base_x1 + 16 * (i + 1))
            b.inc(ve.tensor_reduce(pw_red24[:, i:i + 1], buf[:, :], AX,
                                   ALU.add, apply_absolute_value=True),
                  "free")
        b.w(te, "free")
        b.inc(te.matmul(pp_sum[0:1, :], ocf[:, :], pw_red24[:, :], start=True,
                        stop=True), "mm")
        b.w(ve, "mm")
        b.inc(ve.tensor_copy(rowA[0:1, 0:32], pp_sum[0:1, :]), "v")
        ve.tensor_reduce(srow[0:1, 0:1], rowA[0:1, 0:16], AX, ALU.add)
        b.inc(ve.tensor_reduce(srow[0:1, 1:2], rowA[0:1, 16:24], AX, ALU.add),
              "v")
        b.w(gp, "v")
        b.inc(gp.dma_start(out=ar_s_in[:, :], in_=srow[0:1, 0:2]), "g", 16)
        b.w(gp, "g")
        b.inc(gp.collective_compute("AllReduce", ALU.add, ins=[ar_s_in[:, :]],
                                    outs=[ar_s[:, :]],
                                    replica_groups=[list(range(8))]), "cc")
        b.w(gp, "cc")
        b.inc(gp.dma_start(out=srow[0:1, 2:4], in_=ar_s[:, :]), "g", 16)
        b.w(ve, "g")
        ve.tensor_scalar(srow[0:1, 4:5], srow[0:1, 2:3], 1.0 / WGV_N, 1e-5,
                         ALU.mult, ALU.max)
        b.inc(ve.tensor_scalar(srow[0:1, 5:6], srow[0:1, 3:4], 1.0 / WOUT_N,
                               1e-5, ALU.mult, ALU.max), "v")
        ve.memset(rowA[0:1, 0:128], 1.0)
        ve.tensor_scalar_mul(rowA[0:1, 0:64], rowA[0:1, 0:64], srow[0:1, 4:5])
        ve.tensor_scalar_mul(rowA[0:1, 64:128], rowA[0:1, 64:128],
                             srow[0:1, 5:6])
        b.inc(ve.reciprocal(rowB[0:1, 0:128], rowA[0:1, 0:128]), "v")
        b.w(te, "v")
        b.inc(te.matmul(pp_c[:, 0:8], orf[0:1, :], rowB[0:1, 0:8],
                        start=True, stop=True), "mm")
        b.w(ve, "mm")
        b.inc(ve.tensor_copy(scol[:, 0:8], pp_c[:, 0:8]), "v")
        b.w(te, "v")
        b.inc(te.matmul(pp_c[:, 0:8], orf[0:1, :], rowB[0:1, 64:72],
                        start=True, stop=True), "mm")
        b.w(ve, "mm")
        b.inc(ve.tensor_copy(scol[:, 8:16], pp_c[:, 0:8]), "v")

        if getattr(b, "dump_srow", False):
            b.w(gp, "v")
            b.inc(gp.dma_start(out=b.dump_y[0:1, 0:16], in_=srow[0:1, :]),
                  "stA", 16)
        b.bar()
        def one_block(src_ap, invcol, dst_ap):
            buf = bufs[one_block.i % 2]
            if one_block.i >= 2:
                b.w(sy, "k2s", b.cnt["k2s"] - 1)
            b.inc(sy.dma_start(out=buf[:, :], in_=src_ap), "x1", 16)
            b.w(ve, "x1")
            b.w(ve, "tq")
            ve.tensor_scalar_mul(buf[:, :], buf[:, :], invcol)
            ve.tensor_scalar(buf[:, :], buf[:, :], MAGIC, MAGIC, ALU.add,
                             ALU.subtract)
            b.inc(ve.tensor_scalar(pw_q[:, :], buf[:, :], 1.0, -1.0,
                                   ALU.min, ALU.max), "k2s")
            b.w(te, "k2s")
            b.w(te, "g")
            for grp in range(4):
                b.w(te, "ev")
                for j in range(4):
                    dcx = grp * 4 + j
                    ins = te.transpose(pp_t[:, j * 128:(j + 1) * 128],
                                       pw_q[:, dcx * 128:(dcx + 1) * 128],
                                       idb[:])
                b.inc(ins, "mm")
                if grp == 3:
                    b.inc(te.nop(), "tq")
                b.w(ve, "mm")
                b.inc(ve.tensor_copy(pw_qT[:, grp * 512:(grp + 1) * 512],
                                     pp_t[:, :]), "ev")
            b.w(gp, "ev")
            b.inc(gp.dma_start(out=dst_ap, in_=v3(pw_qT[:, :], 16, 128)),
                  "g", 16)
            one_block.i += 1
        one_block.i = 0

        for ob in range(16):
            dst = ag_wgv_in.rearrange("(dc p) o -> p dc o", p=128)[
                :, :, ob * 128:(ob + 1) * 128]
            one_block(wgv_c[ob * 128:(ob + 1) * 128, :], scol[:, 0:1], dst)
        for pb in range(2):
            for og in range(4):
                dst = ag_wout_in.rearrange("(og oc p) m -> p og oc m",
                                           og=4, oc=16)[
                    :, og, :, pb * 128:(pb + 1) * 128]
                one_block(wout_c[pb * 128:(pb + 1) * 128,
                                 og * 2048:(og + 1) * 2048], scol[:, 8:9], dst)
        b.w(gp, "g")
        b.inc(gp.collective_compute("AllGather", ALU.bypass,
                                    ins=[ag_wgv_in[:, :]], outs=[ag_wgv[:, :]],
                                    replica_groups=[list(range(8))]), "cc")
        b.inc(gp.collective_compute("AllGather", ALU.bypass,
                                    ins=[ag_wout_in[:, :]], outs=[ag_wout[:, :]],
                                    replica_groups=[list(range(8))]), "cc")
    b.bar()


def emit_xload(b, x_c, xT):
    nc = b.nc
    sy, te, ve = nc.sync, nc.tensor, nc.vector
    idf = b.ten["idf"]
    with (
        nc.sbuf_tensor("xlA", [128, 2048], F32) as xlA,
        nc.sbuf_tensor("xlB", [128, 2048], F32) as xlB,
        nc.psum_tensor("px_t", [128, 512], F32) as pt,
    ):
        bufs = [xlA, xlB]
        base_x1 = b.cnt["x1"]
        base_free = b.cnt["free"]
        for tt in range(8):
            buf = bufs[tt % 2]
            b.w(sy, "free")
            b.inc(sy.dma_start(out=buf[:, :],
                               in_=x_c[tt * 128:(tt + 1) * 128, :]), "x1", 16)
            b.w(te, "x1")
            for grp in range(4):
                b.w(te, "ev")
                for j in range(4):
                    dc = grp * 4 + j
                    ins = te.transpose(pt[:, j * 128:(j + 1) * 128],
                                       buf[:, dc * 128:(dc + 1) * 128], idf[:])
                b.inc(ins, "mm")
                if grp == 3:
                    b.inc(te.nop(), "free")
                b.w(ve, "mm")
                for j in range(4):
                    dc = grp * 4 + j
                    ins = ve.tensor_copy(
                        xT[:, dc * TPC + tt * 128:dc * TPC + (tt + 1) * 128],
                        pt[:, j * 128:(j + 1) * 128])
                b.inc(ins, "ev")


def emit_stats(b, xT, mean_row, rstd_row, tag):
    nc = b.nc
    te, ve, sc = nc.tensor, nc.vector, nc.scalar
    ocf, ceps, rowC = b.ten["ocf"], b.ten["ceps"], b.ten["rowC"]
    with (
        nc.sbuf_tensor(f"sq_{tag}", [128, 2 * TPC], F32) as sqt,
        nc.psum_tensor(f"psx_{tag}", [1, TPC], F32) as psx,
        nc.psum_tensor(f"psq_{tag}", [1, TPC], F32) as psq,
    ):
        b.w(sc, "ev")
        b.w(te, "ev")
        b.w(ve, "ev")
        base_mm = b.cnt["mm"]
        for dc in range(DC):
            xdc = xT[:, dc * TPC:(dc + 1) * TPC]
            sq = sqt[:, (dc % 2) * TPC:(dc % 2 + 1) * TPC]
            if dc >= 2:
                sc.wait_ge(b.sem["mm"], base_mm + dc - 1)
            b.inc(sc.activation(sq[:, :], xdc, ACT.Square), "sc")
            for th in range(2):
                te.matmul(psx[0:1, th * 512:(th + 1) * 512], ocf[:, :],
                          xdc[:, th * 512:(th + 1) * 512],
                          start=(dc == 0), stop=(dc == DC - 1))
            b.w(te, "sc")
            for th in range(2):
                ins = te.matmul(psq[0:1, th * 512:(th + 1) * 512], ocf[:, :],
                                sq[:, th * 512:(th + 1) * 512],
                                start=(dc == 0), stop=(dc == DC - 1))
            b.inc(ins, "mm")
        b.w(ve, "mm")
        ve.tensor_scalar_mul(mean_row[0:1, :], psx[0:1, :], 1.0 / D)
        ve.tensor_scalar_mul(rstd_row[0:1, :], psq[0:1, :], 1.0 / D)
        ve.tensor_mul(rowC[0:1, :], mean_row[0:1, :], mean_row[0:1, :])
        b.inc(ve.tensor_sub(rstd_row[0:1, :], rstd_row[0:1, :], rowC[0:1, :]),
              "v")
        b.w(sc, "v")
        b.inc(sc.activation(rstd_row[0:1, :], rstd_row[0:1, :], ACT.Sqrt,
                            bias=ceps[0:1, :]), "sc")
        b.w(ve, "sc")
        b.inc(ve.reciprocal(rstd_row[0:1, :], rstd_row[0:1, :]), "v")


def bcast_row(b, row_slice, dst_tile, psb):
    nc = b.nc
    te, ve = nc.tensor, nc.vector
    orf = b.ten["orf"]
    n = row_slice.shape[1]
    nt = (n + 511) // 512
    for th in range(nt):
        w_ = min(512, n - th * 512)
        b.w(te, "v")
        b.inc(te.matmul(psb[:, 0:w_], orf[0:1, :],
                        row_slice[0:1, th * 512:th * 512 + w_],
                        start=True, stop=True), "mm")
        b.w(ve, "mm")
        b.inc(ve.tensor_copy(dst_tile[:, th * 512:th * 512 + w_],
                             psb[:, 0:w_]), "v")


def rowify(b, acc, out_row, psT, psR):
    nc = b.nc
    te, ve = nc.tensor, nc.vector
    idf, gmc = b.ten["idf"], b.ten["gmc"]
    for j in range(8):
        b.w(te, "v")
        b.inc(te.transpose(psT[:, (j % 4) * 128:(j % 4 + 1) * 128],
                           acc[:, j * 128:(j + 1) * 128], idf[:]), "mm")
        b.w(ve, "mm")
        b.inc(ve.tensor_reduce(gmc[:, j:j + 1],
                               psT[:, (j % 4) * 128:(j % 4 + 1) * 128],
                               AX, ALU.max), "v")
    b.w(te, "v")
    for j in range(8):
        ins = te.transpose(psR[0:1, j * 128:(j + 1) * 128], gmc[:, j:j + 1],
                           idf[:])
    b.inc(ins, "mm")
    b.w(ve, "mm")
    b.inc(ve.tensor_copy(out_row[0:1, :], psR[0:1, :]), "v")


def emit_ffn(b, xT, tag, sub=0):
    b.sub = sub
    import os
    b.sub42_slot = int(os.environ.get("SUB42_SLOT", "0"))
    b.ffn_abort = False
    nc = b.nc
    sy, gp, te, ve, sc = nc.sync, nc.gpsimd, nc.tensor, nc.vector, nc.scalar
    t_ = b.ten
    rowA, rowB, rowC, rowD = t_["rowA"], t_["rowB"], t_["rowC"], t_["rowD"]
    mbt, rbt, fbt, cbt = t_["mbt"], t_["rbt"], t_["fbt"], t_["cbt"]
    nw_pp, srow, ceps, ocf = t_["nw_pp"], t_["srow"], t_["ceps"], t_["ocf"]
    u_hbm, k2_hbm = b.hbm["u_hbm"], b.hbm["k2_hbm"]
    ag_wgv, ag_wout = b.hbm["ag_wgv"], b.hbm["ag_wout"]

    with nc.sbuf_tensor(f"k1_{tag}", [128, DC * TPC], BF16) as k1:
        b.bar()
        emit_stats(b, xT, rowA, rowB, tag)
        b.bar()
        with nc.psum_tensor(f"psb_{tag}", [128, 512], F32) as psb:
            bcast_row(b, rowA[0:1, :], mbt, psb)
            bcast_row(b, rowB[0:1, :], rbt, psb)
        b.bar()
        with (
            nc.sbuf_tensor(f"ln0_{tag}", [128, 2 * TPC], F32) as ln0t,
            nc.sbuf_tensor(f"gt_{tag}", [128, TPC], F32) as gt,
            nc.sbuf_tensor(f"sq1_{tag}", [128, 2 * TPC], F32) as sqt,
            nc.sbuf_tensor(f"acc_{tag}", [128, TPC], F32) as acc,
            nc.psum_tensor(f"psq2_{tag}", [1, TPC], F32) as psq2,
        ):
            ve.memset(acc[:, :], 0.0)
            base_mm = b.cnt["mm"]
            base_sc = b.cnt["sc"]
            for dc in range(DC):
                xdc = xT[:, dc * TPC:(dc + 1) * TPC]
                ln0 = ln0t[:, (dc % 2) * TPC:(dc % 2 + 1) * TPC]
                sq = sqt[:, (dc % 2) * TPC:(dc % 2 + 1) * TPC]
                if dc >= 2:
                    ve.wait_ge(b.sem["sc"], base_sc + dc - 1)
                ve.tensor_sub(ln0[:, :], xdc, mbt[:, :])
                ve.tensor_mul(ln0[:, :], ln0[:, :], rbt[:, :])
                ve.tensor_scalar_mul(gt[:, :], ln0[:, :], nw_pp[:, dc:dc + 1])
                ve.tensor_max(acc[:, :], acc[:, :], gt[:, :])
                ve.scalar_tensor_tensor(acc[:, :], gt[:, :], -1.0, acc[:, :],
                                        ALU.mult, ALU.max)
                b.inc(ve.scalar_tensor_tensor(xdc, ln0[:, :], 0.5, xdc,
                                              ALU.mult, ALU.add), "v")
                sc.wait_ge(b.sem["v"], b.cnt["v"])
                b.inc(sc.activation(sq[:, :], ln0[:, :], ACT.Square), "sc")
                te.wait_ge(b.sem["sc"], base_sc + dc + 1)
                for th in range(2):
                    ins = te.matmul(psq2[0:1, th * 512:(th + 1) * 512],
                                    ocf[:, :], sq[:, th * 512:(th + 1) * 512],
                                    start=(dc == 0), stop=(dc == DC - 1))
                b.inc(ins, "mm")
            if sub == 1:
                return
            b.w(ve, "mm")
            b.inc(ve.tensor_scalar_mul(rowC[0:1, :], psq2[0:1, :], 1.0 / D),
                  "v")
            b.w(sc, "v")
            b.inc(sc.activation(rowC[0:1, :], rowC[0:1, :], ACT.Sqrt,
                                bias=ceps[0:1, :]), "sc")
            b.w(ve, "sc")
            b.inc(ve.reciprocal(rowC[0:1, :], rowC[0:1, :]), "v")
            b.bar()
            with (
                nc.psum_tensor(f"psT_{tag}", [128, 512], F32) as psT,
                nc.psum_tensor(f"psR_{tag}", [1, TPC], F32) as psR,
            ):
                rowify(b, acc, rowD, psT, psR)
            b.bar()
            ve.tensor_mul(rowD[0:1, :], rowD[0:1, :], rowC[0:1, :])
            ve.tensor_scalar_max(rowD[0:1, :], rowD[0:1, :], 1e-5)
            ve.tensor_scalar_mul(rowA[0:1, :], rowD[0:1, :], 1.0 / 127.0)
            ve.tensor_scalar_mul(rowA[0:1, :], rowA[0:1, :], srow[0:1, 4:5])
            ve.reciprocal(rowD[0:1, :], rowD[0:1, :])
            ve.tensor_mul(rowD[0:1, :], rowD[0:1, :], rowC[0:1, :])
            ve.tensor_scalar_mul(rowD[0:1, :], rowD[0:1, :], 127.0)
            ve.tensor_scalar(rowC[0:1, :], rowB[0:1, :], 0.5, 1.0, ALU.mult,
                             ALU.add)
            ve.reciprocal(rowC[0:1, :], rowC[0:1, :])
            ve.tensor_mul(rowC[0:1, :], rowC[0:1, :], rowB[0:1, :])
            b.inc(ve.tensor_mul(rowD[0:1, :], rowD[0:1, :], rowC[0:1, :]), "v")
            with nc.psum_tensor(f"psb2_{tag}", [128, 512], F32) as psb:
                bcast_row(b, rowD[0:1, :], fbt, psb)
                bcast_row(b, rowA[0:1, :], cbt, psb)
            b.bar()
            if sub == 21:
                b.w(ve, "ev")
                ve.tensor_copy(xT[:, 0:TPC], fbt[:, :])
                ve.tensor_copy(xT[:, TPC:2 * TPC], cbt[:, :])
                ve.tensor_copy(xT[:, 2 * TPC:3 * TPC], mbt[:, :])
                b.inc(ve.tensor_copy(xT[:, 3 * TPC:4 * TPC], rbt[:, :]), "ev")
                return
            for dc in range(DC):
                xdc = xT[:, dc * TPC:(dc + 1) * TPC]
                tmp = ln0t[:, (dc % 2) * TPC:(dc % 2 + 1) * TPC]
                ve.tensor_sub(tmp[:, :], xdc, mbt[:, :])
                ve.tensor_scalar_mul(tmp[:, :], tmp[:, :], nw_pp[:, dc:dc + 1])
                ve.tensor_mul(tmp[:, :], tmp[:, :], fbt[:, :])
                ve.tensor_scalar(tmp[:, :], tmp[:, :], MAGIC, MAGIC, ALU.add,
                                 ALU.subtract)
                b.inc(ve.tensor_scalar(k1[:, dc * TPC:(dc + 1) * TPC],
                                       tmp[:, :], 127.0, -127.0,
                                       ALU.min, ALU.max), "v")
            if sub == 2:
                for dc in range(DC):
                    b.inc(ve.tensor_copy(xT[:, dc * TPC:(dc + 1) * TPC],
                                         k1[:, dc * TPC:(dc + 1) * TPC]), "ev")
                return

        b.bar()
        with (
            nc.sbuf_tensor(f"vs_{tag}", [128, 2 * DC * 256], BF16) as vslab,
            nc.sbuf_tensor(f"gs_{tag}", [128, 2 * DC * 256], BF16) as gslab,
            nc.sbuf_tensor(f"sg_{tag}", [128, 2 * TPC], F32) as sgt,
            nc.sbuf_tensor(f"ut_{tag}", [128, 2 * TPC], F32) as ut,
            nc.sbuf_tensor(f"acc2_{tag}", [128, TPC], F32) as acc2,
        ):
            b.w(sy, "cc", 3)
            ve.memset(acc2[:, :], 0.0)
            with (
                nc.psum_tensor(f"pv_{tag}", [128, 2 * TPC], F32) as pv,
                nc.psum_tensor(f"pg_{tag}", [128, 2 * TPC], F32) as pg,
            ):
                base_slab = b.cnt["slab"]
                base_free = b.cnt["free"]
                base_ps = b.cnt["ps"]
                base_ev = b.cnt["ev"]
                base_ug = [b.cnt["ugA"], b.cnt["ugB"]]
                base_sc = b.cnt["sc"]
                for w_ in range(OC):
                    if b.sub == 35 and w_ >= 2:
                        break
                    sl, pp = w_ // 2, w_ % 2
                    if w_ % 2 == 0:
                        vdst = vslab[:, (sl % 2) * DC * 256:
                                     (sl % 2 + 1) * DC * 256]
                        gdst = gslab[:, (sl % 2) * DC * 256:
                                     (sl % 2 + 1) * DC * 256]
                        if sl >= 2:
                            b.w(sy, "ev", base_ev + 2 * sl - 2)
                        ov, og = sl * 256, HID + sl * 256
                        sv, cv = ov // 2048, ov % 2048
                        sg_, cg = og // 2048, og % 2048
                        b.inc(sy.dma_start(
                            out=v3(vdst[:, :], 16, 256),
                            in_=ag_wgv[sv * 2048:(sv + 1) * 2048,
                                       cv:cv + 256].rearrange(
                                "(dc p) o -> p dc o", p=128)), "slab", 16)
                        b.inc(sy.dma_start(
                            out=v3(gdst[:, :], 16, 256),
                            in_=ag_wgv[sg_ * 2048:(sg_ + 1) * 2048,
                                       cg:cg + 256].rearrange(
                                "(dc p) o -> p dc o", p=128)), "slab", 16)
                    vsl = vslab[:, (sl % 2) * DC * 256:(sl % 2 + 1) * DC * 256]
                    gsl = gslab[:, (sl % 2) * DC * 256:(sl % 2 + 1) * DC * 256]
                    if b.sub == 35 and w_ == 0:
                        gp.wait_ge(b.sem["slab"], b.cnt["slab"])
                        for j in range(2):
                            b.inc(gp.dma_start(
                                out=b.y_dbg[j * 128:(j + 1) * 128, :],
                                in_=vslab[:, j * 2048:(j + 1) * 2048]), "stA", 16)
                            b.inc(gp.dma_start(
                                out=b.y_dbg[256 + j * 128:256 + (j + 1) * 128, :],
                                in_=gslab[:, j * 2048:(j + 1) * 2048]), "stA", 16)
                            b.inc(gp.dma_start(
                                out=b.y_dbg[512 + j * 128:512 + (j + 1) * 128, :],
                                in_=vslab[:, 4096 + j * 2048:4096 + (j + 1) * 2048]), "stA", 16)
                            b.inc(gp.dma_start(
                                out=b.y_dbg[768 + j * 128:768 + (j + 1) * 128, :],
                                in_=gslab[:, 4096 + j * 2048:4096 + (j + 1) * 2048]), "stA", 16)
                    te.wait_ge(b.sem["slab"], base_slab + 32 * (sl + 1))
                    if w_ >= 2:
                        te.wait_ge(b.sem["ev"], base_ev + w_ - 1)
                    pvw = pv[:, pp * TPC:(pp + 1) * TPC]
                    pgw = pg[:, pp * TPC:(pp + 1) * TPC]
                    off = (w_ % 2) * 128
                    for th in range(2):
                        for dc in range(DC):
                            te.matmul(pvw[:, th * 512:(th + 1) * 512],
                                      vsl[:, dc * 256 + off:dc * 256 + off + 128],
                                      k1[:, dc * TPC + th * 512:
                                         dc * TPC + (th + 1) * 512],
                                      start=(dc == 0), stop=(dc == DC - 1))
                    for th in range(2):
                        for dc in range(DC):
                            ins = te.matmul(
                                pgw[:, th * 512:(th + 1) * 512],
                                gsl[:, dc * 256 + off:dc * 256 + off + 128],
                                k1[:, dc * TPC + th * 512:
                                   dc * TPC + (th + 1) * 512],
                                start=(dc == 0), stop=(dc == DC - 1))
                    b.inc(ins, "ps")
                    if w_ % 2 == 1:
                        b.inc(te.nop(), "free")
                    sg = sgt[:, pp * TPC:(pp + 1) * TPC]
                    uu = ut[:, pp * TPC:(pp + 1) * TPC]
                    ve.wait_ge(b.sem["ps"], base_ps + w_ + 1)
                    if w_ >= 2:
                        ve.wait_ge(b.sem[["ugA", "ugB"][w_ % 2]],
                                   base_ug[w_ % 2] + 16 * (w_ // 2))
                    b.inc(ve.tensor_mul(sg[:, :], pgw[:, :], cbt[:, :]), "aux")
                    sc.wait_ge(b.sem["aux"], b.cnt["aux"])
                    b.inc(sc.activation(sg[:, :], sg[:, :], ACT.Silu), "sc")
                    ve.wait_ge(b.sem["sc"], base_sc + w_ + 1)
                    if b.sub == 32:
                        ve.tensor_scalar_mul(uu[:, :], pvw[:, :], 1.0)
                    elif b.sub == 33:
                        ve.tensor_scalar_mul(uu[:, :], pgw[:, :], 1.0)
                    else:
                        ve.tensor_mul(uu[:, :], pvw[:, :], sg[:, :])
                    ve.tensor_max(acc2[:, :], acc2[:, :], uu[:, :])
                    ve.scalar_tensor_tensor(acc2[:, :], uu[:, :], -1.0,
                                            acc2[:, :], ALU.mult, ALU.max)
                    b.inc(ve.engine_nop(), "ev")
                    gp.wait_ge(b.sem["ev"], base_ev + w_ + 1)
                    b.inc(gp.dma_start(out=u_hbm[w_, :, :], in_=uu[:, :]),
                          ["ugA", "ugB"][w_ % 2], 16)
            if getattr(b, "sub", 0) == 35:
                b.w(gp, "stA")
                return
            if getattr(b, "sub", 0) in (31, 32, 33):
                b.w(gp, "ugA")
                b.w(gp, "ugB")
                b.inc(gp.dma_start(
                    out=b.y_dbg.rearrange("a (c d) -> (a c) d", c=2),
                    in_=u_hbm.rearrange("oc p t -> (oc p) t")[0:2048, :]),
                    "stA", 16)
                b.w(gp, "stA")
                return
            b.bar()
            with (
                nc.psum_tensor(f"psT2_{tag}", [128, 512], F32) as psT,
                nc.psum_tensor(f"psR2_{tag}", [1, TPC], F32) as psR,
            ):
                rowify(b, acc2, rowD, psT, psR)
            b.bar()
            ve.tensor_mul(rowD[0:1, :], rowD[0:1, :], rowA[0:1, :])
            ve.tensor_scalar_max(rowD[0:1, :], rowD[0:1, :], 1e-5)
            ve.tensor_scalar_mul(rowB[0:1, :], rowD[0:1, :], 0.5 / 127.0)
            b.inc(ve.tensor_scalar_mul(rowB[0:1, :], rowB[0:1, :],
                                       srow[0:1, 5:6]), "v")
            ve.reciprocal(rowD[0:1, :], rowD[0:1, :])
            ve.tensor_mul(rowD[0:1, :], rowD[0:1, :], rowA[0:1, :])
            b.inc(ve.tensor_scalar_mul(rowD[0:1, :], rowD[0:1, :], 127.0), "v")
            with nc.psum_tensor(f"psb3_{tag}", [128, 512], F32) as psb:
                bcast_row(b, rowD[0:1, :], fbt, psb)
                bcast_row(b, rowB[0:1, :], cbt, psb)
            if sub == 3:
                b.w(ve, "ev")
                for dc in range(DC):
                    b.inc(ve.tensor_copy(xT[:, dc * TPC:(dc + 1) * TPC],
                                         fbt[:, :] if dc == 0 else
                                         (cbt[:, :] if dc == 1 else
                                          acc2[:, :])), "ev")
                return
        with (
            nc.sbuf_tensor(f"uq_{tag}", [128, 2 * TPC], F32) as uq,
            nc.sbuf_tensor(f"k2t_{tag}", [128, 2 * TPC], BF16) as k2t,
        ):
            b.w(sy, "ugA")
            b.w(sy, "ugB")
            base_x1 = b.cnt["x1"]
            base_k2 = b.cnt["k2s"]
            base_kpp = [b.cnt["k2A"], b.cnt["k2B"]]
            for oc in range(OC):
                buf = uq[:, (oc % 2) * TPC:(oc % 2 + 1) * TPC]
                kb = k2t[:, (oc % 2) * TPC:(oc % 2 + 1) * TPC]
                if oc >= 2:
                    sy.wait_ge(b.sem["k2s"], base_k2 + oc - 1)
                b.inc(sy.dma_start(out=buf[:, :], in_=u_hbm[oc, :, :]),
                      "x1", 16)
                ve.wait_ge(b.sem["x1"], base_x1 + 16 * (oc + 1))
                if oc >= 2:
                    ve.wait_ge(b.sem[["k2A", "k2B"][oc % 2]],
                               base_kpp[oc % 2] + 16 * (oc // 2))
                ve.tensor_mul(buf[:, :], buf[:, :], fbt[:, :])
                ve.tensor_scalar(buf[:, :], buf[:, :], MAGIC, MAGIC, ALU.add,
                                 ALU.subtract)
                b.inc(ve.tensor_scalar(kb[:, :], buf[:, :], 127.0, -127.0,
                                       ALU.min, ALU.max), "k2s")
                gp.wait_ge(b.sem["k2s"], base_k2 + oc + 1)
                b.inc(gp.dma_start(out=k2_hbm[oc, :, :], in_=kb[:, :]),
                      ["k2A", "k2B"][oc % 2], 16)
        if sub == 41:
            b.w(sy, "k2A")
            b.w(sy, "k2B")
            with nc.sbuf_tensor(f"k2d_{tag}", [128, 2048], BF16) as k2d:
                for blk in range(8):
                    b.w(sy, "ev")
                    b.inc(sy.dma_start(
                        out=v3(k2d[:, :], 4, 512),
                        in_=k2_hbm.rearrange("oc p t -> p oc t")[
                            :, blk * 4:(blk + 1) * 4, 0:512]), "x1", 16)
                    b.w(ve, "x1")
                    b.inc(ve.tensor_copy(
                        xT[:, blk * 2048:(blk + 1) * 2048], k2d[:, :]), "ev")
            return
        b.bar()
        with (
            nc.sbuf_tensor(f"k2h_{tag}", [128, 16 * TPC], BF16) as k2h,
            nc.sbuf_tensor(f"m2s_{tag}", [128, 2 * 16 * 256], BF16) as m2s,
            nc.sbuf_tensor(f"m2e_{tag}", [128, TPC], F32) as m2e,
            nc.psum_tensor(f"pm2_{tag}", [128, 4 * TPC], F32) as pm2,
        ):
            nslab = 0
            base_slab = b.cnt["slab"]
            base_free = b.cnt["free"]
            base_ps = b.cnt["ps"]
            base_ev = b.cnt["ev"]
            base_x2 = b.cnt["x2"]
            for oq in range(4):
                b.w(sy, "k2A")
                b.w(sy, "k2B")
                if oq >= 1:
                    b.w(sy, "ev", base_ev + 8 * oq)
                b.inc(sy.dma_start(
                    out=v3(k2h[:, :], 16, TPC),
                    in_=k2_hbm.rearrange("(oqq oc) p t -> p oqq oc t", oqq=4)[
                        :, oq, :, :]), "x2", 16)
                for slot in range(8):
                    sb_ = m2s[:, (nslab % 2) * 16 * 256:
                              (nslab % 2 + 1) * 16 * 256]
                    if nslab >= 2:
                        b.w(sy, "ev", base_ev + nslab - 1)
                    b.inc(sy.dma_start(
                        out=v3(sb_[:, :], 16, 256),
                        in_=ag_wout.rearrange("(s oc p) m -> p s oc m", s=8,
                                              oc=64)[
                            :, slot, oq * 16:(oq + 1) * 16, :]), "slab", 16)
                    te.wait_ge(b.sem["slab"], b.cnt["slab"])
                    te.wait_ge(b.sem["x2"], base_x2 + 16 * (oq + 1))
                    if nslab >= 2:
                        te.wait_ge(b.sem["ev"], base_ev + nslab - 1)
                    ppw = nslab % 2
                    for dml in range(2):
                        pmw = pm2[:, (ppw * 2 + dml) * TPC:
                                  (ppw * 2 + dml + 1) * TPC]
                        for th in range(2):
                            for ocl in range(16):
                                ins = te.matmul(
                                    pmw[:, th * 512:(th + 1) * 512],
                                    sb_[:, ocl * 256 + dml * 128:
                                        ocl * 256 + dml * 128 + 128],
                                    k2h[:, ocl * TPC + th * 512:
                                        ocl * TPC + (th + 1) * 512],
                                    start=(ocl == 0), stop=(ocl == 15))
                    ins = te.matmul(pm2[:, (ppw * 2 + 1) * TPC + 512:
                                        (ppw * 2 + 2) * TPC],
                                    b.ten["zerot"][:, :], k2h[:, 0:512],
                                    start=False, stop=True)
                    b.inc(ins, "ps")
                    b.inc(te.nop(), "free")
                    ve.wait_ge(b.sem["ps"], base_ps + nslab + 1)
                    if b.sub == 42 and nslab == b.sub42_slot:
                        ppx = b.sub42_slot % 2
                        ve.tensor_copy(xT[:, 0:TPC],
                                       pm2[:, (ppx * 2 + 1) * TPC:
                                           (ppx * 2 + 2) * TPC])
                        b.inc(ve.tensor_copy(xT[:, TPC:2 * TPC],
                                             pm2[:, ppx * 2 * TPC:
                                                 (ppx * 2 + 1) * TPC]), "ev")
                    for dml in range(2):
                        dmc = slot * 2 + dml
                        pmw = pm2[:, (ppw * 2 + dml) * TPC:
                                  (ppw * 2 + dml + 1) * TPC]
                        ve.tensor_mul(m2e[:, :], pmw[:, :], cbt[:, :])
                        ins = ve.tensor_add(
                            xT[:, dmc * TPC:(dmc + 1) * TPC],
                            xT[:, dmc * TPC:(dmc + 1) * TPC], m2e[:, :])
                    b.inc(ins, "ev")
                    nslab += 1
                    if b.sub == 42 and nslab > b.sub42_slot:
                        break
                if b.sub == 42 and nslab > b.sub42_slot:
                    break
            b.w(te, "ev")

def emit_ln_stage(b, xT):
    nc = b.nc
    sy, gp, ve = nc.sync, nc.gpsimd, nc.vector
    t_ = b.ten
    rowA, rowB, mbt, rbt, fbt = (t_["rowA"], t_["rowB"], t_["mbt"], t_["rbt"],
                                 t_["fbt"])
    ag_ln_in, ag_ln, xsp = b.hbm["ag_ln_in"], b.hbm["ag_ln"], b.hbm["xsp"]
    b.bar()
    emit_stats(b, xT, rowA, rowB, "ln1")
    b.bar()
    with nc.psum_tensor("psb_ln", [128, 512], F32) as psb:
        bcast_row(b, rowA[0:1, :], mbt, psb)
        bcast_row(b, rowB[0:1, :], rbt, psb)
    b.bar()
    with nc.sbuf_tensor("lnb", [128, DC * TPC], BF16) as lnb:
        for dc in range(DC):
            xdc = xT[:, dc * TPC:(dc + 1) * TPC]
            ve.tensor_sub(fbt[:, :], xdc, mbt[:, :])
            b.inc(ve.tensor_mul(lnb[:, dc * TPC:(dc + 1) * TPC], fbt[:, :],
                                rbt[:, :]), "v")
        b.w(gp, "v")
        b.inc(gp.dma_start(
            out=ag_ln_in.rearrange("(dc p) t -> p dc t", p=128),
            in_=v3(lnb[:, :], 16, TPC)), "g", 16)
        b.inc(gp.dma_start(out=xsp[:, :], in_=xT[:, :]), "g", 16)
        b.w(gp, "g")
        b.inc(gp.collective_compute("AllGather", ALU.bypass,
                                    ins=[ag_ln_in[:, :]], outs=[ag_ln[:, :]],
                                    replica_groups=[list(range(8))]), "cc")
        b.w(gp, "cc")


def _attn_qkv_tb(b, C, tb):
    import os
    nc = b.nc
    sy, gp, te, ve, sc = nc.sync, nc.gpsimd, nc.tensor, nc.vector, nc.scalar
    rank, off = tb // 2, (tb % 2) * 512
    lsl = C.lns[:, (tb % 2) * DC * 512:(tb % 2 + 1) * DC * 512]
    if tb >= 2:
        b.w(sy, "ev", C.base_ev + 4 * (tb - 1))
        b.w(sy, "sc", C.base_sc + 2 * (tb - 1))
    b.inc(sy.dma_start(
        out=v3(lsl[:, :], 16, 512),
        in_=C.ag_ln.rearrange("(r dc p) t -> p r dc t", r=8, dc=16)[
            :, rank, :, off:off + 512]), "slab", 16)
    te.wait_ge(b.sem["slab"], C.base_slab + 16 * (tb + 1))
    te.wait_ge(b.sem["g"], b.cnt["g"])
    if os.environ.get("ATTN_SLABONLY"):
        b.inc(te.nop(), "free")
        return
    for wsb, dst, bias, scale in ((C.wqb, C.Qh, C.bqs_pp, RSQ),
                                  (C.wkb, C.Kh, C.bk_pp, 1.0)):
        b.w(te, "sc")
        for h2 in range(2):
            for dc in range(DC):
                ins = te.matmul(
                    C.pq[:, h2 * 512:(h2 + 1) * 512],
                    wsb[:, dc * 256 + h2 * 128:dc * 256 + h2 * 128 + 128],
                    lsl[:, dc * 512:(dc + 1) * 512],
                    start=(dc == 0), stop=(dc == DC - 1))
        b.inc(ins, "ps")
        sc.wait_ge(b.sem["ps"], b.cnt["ps"])
        for h2 in range(2):
            ins = sc.activation(
                dst[:, h2 * T + tb * 512:h2 * T + (tb + 1) * 512],
                C.pq[:, h2 * 512:(h2 + 1) * 512], ACT.Identity,
                bias=bias[:, h2:h2 + 1], scale=scale)
        b.inc(ins, "sc")
    if os.environ.get("ATTN_NOV"):
        b.inc(te.nop(), "free")
        return
    for tt in range(4):
        te.wait_ge(b.sem["ev"], max(0, b.cnt["ev"] - 1))
        for dc in range(DC):
            ins = te.matmul(
                C.pvv[:, (tt % 2) * 512:(tt % 2) * 512 + 256],
                lsl[:, dc * 512 + tt * 128:dc * 512 + tt * 128 + 128],
                C.wvb[:, dc * 256:(dc + 1) * 256],
                start=(dc == 0), stop=(dc == DC - 1))
        b.inc(ins, "ps")
        ve.wait_ge(b.sem["ps"], b.cnt["ps"])
        Tc = tb * 4 + tt
        b.inc(ve.tensor_add(C.Vh[:, Tc * 256:(Tc + 1) * 256],
                            C.pvv[:, (tt % 2) * 512:(tt % 2) * 512 + 256],
                            C.bvb[:, :]), "ev")
    b.inc(te.nop(), "free")
    if os.environ.get("ATTN_NOSTATS"):
        return
    b.w(te, "sc")
    spart = int(os.environ.get("ATTN_STATSPART", "0"))
    for h2 in range(2):
        qsl = C.Qh[:, h2 * T + tb * 512:h2 * T + (tb + 1) * 512]
        ksl = C.Kh[:, h2 * T + tb * 512:h2 * T + (tb + 1) * 512]
        if spart != 2:
            b.w(te, "v")
            b.inc(te.matmul(C.pr1[0:1, :], C.bk_cb[:, h2:h2 + 1], qsl,
                            start=True, stop=True), "mm")
            b.inc(te.matmul(C.pr2[0:1, :], C.bqs_cb[:, h2:h2 + 1], ksl,
                            start=True, stop=True), "mm")
            b.w(ve, "mm")
            ve.tensor_reduce(C.st(h2, 0)[0:1, tb:tb + 1], C.pr1[0:1, :],
                             AX, ALU.max)
            b.inc(ve.tensor_reduce(C.st(h2, 2)[0:1, tb:tb + 1],
                                   C.pr2[0:1, :], AX, ALU.max), "v")
        if spart == 1:
            continue
        if spart == 2:
            b.w(te, "v")
        for sl_, bias_, j_ in ((qsl, C.bqs_pp, 1), (ksl, C.bk_pp, 3)):
            ve.tensor_scalar(C.atmp[:, :], sl_, bias_[:, h2:h2 + 1], None,
                             ALU.subtract)
            b.inc(ve.tensor_mul(C.atmp[:, :], C.atmp[:, :], C.atmp[:, :]),
                  "aux")
            te.wait_ge(b.sem["aux"], b.cnt["aux"])
            b.inc(te.matmul(C.pr1[0:1, :], C.ocf[:, :], C.atmp[:, :],
                            start=True, stop=True), "mm")
            b.w(ve, "mm")
            b.inc(ve.tensor_reduce(C.st(h2, j_)[0:1, tb:tb + 1],
                                   C.pr1[0:1, :], AX, ALU.max), "v")


def _attn_sweep_qi(b, C, h2, qi):
    nc = b.nc
    te, ve, sc = nc.tensor, nc.vector, nc.scalar
    nmh = C.nmhall[:, h2 * 16 + qi:h2 * 16 + qi + 1]
    base_ps = b.cnt["ps"]
    base_sc2 = b.cnt["sc"]
    base_ev = b.cnt["ev"]
    sc.wait_ge(b.sem["v"], b.cnt["v"])
    ngrp = 0
    for kvb in range(qi + 1):
        for grp in range(2):
            ppg = ngrp % 2
            plw = C.pl[:, ppg * 1024:(ppg + 1) * 1024]
            pbw = C.pbuf[:, ppg * 1024:(ppg + 1) * 1024]
            if ngrp >= 2:
                te.wait_ge(b.sem["sc"], base_sc2 + ngrp - 1)
            for s2_ in range(2):
                s_ = grp * 2 + s2_
                ins = te.matmul(
                    plw[:, s2_ * 512:(s2_ + 1) * 512],
                    C.Kh[:, h2 * T + kvb * 512 + s_ * 128:
                         h2 * T + kvb * 512 + (s_ + 1) * 128],
                    C.Qh[:, h2 * T + qi * 512:h2 * T + (qi + 1) * 512],
                    start=True, stop=True)
            b.inc(ins, "ps")
            if kvb == qi:
                ve.wait_ge(b.sem["ps"], base_ps + ngrp + 1)
                for s2_ in range(2):
                    s_ = grp * 2 + s2_
                    ins = ve.tensor_add(
                        plw[:, s2_ * 512:(s2_ + 1) * 512],
                        plw[:, s2_ * 512:(s2_ + 1) * 512],
                        C.dmask[:, s_ * 512:(s_ + 1) * 512])
                b.inc(ins, "aux")
                sc.wait_ge(b.sem["aux"], b.cnt["aux"])
            else:
                sc.wait_ge(b.sem["ps"], base_ps + ngrp + 1)
            if ngrp >= 2:
                sc.wait_ge(b.sem["ev"], base_ev + ngrp - 1)
            b.inc(sc.activation(pbw[:, :], plw[:, :], ACT.Exp, bias=nmh),
                  "sc")
            te.wait_ge(b.sem["sc"], base_sc2 + ngrp + 1)
            first = (ngrp == 0)
            last = (kvb == qi and grp == 1)
            for s2_ in range(2):
                s_ = grp * 2 + s2_
                Tc = kvb * 4 + s_
                te.matmul(C.pO[:, :],
                          C.Vh[:, Tc * 256 + h2 * 128:Tc * 256 +
                               (h2 + 1) * 128],
                          pbw[:, s2_ * 512:(s2_ + 1) * 512],
                          start=(first and s2_ == 0),
                          stop=(last and s2_ == 1))
                ins = te.matmul(C.pd[0:1, :], C.ocb[:, :],
                                pbw[:, s2_ * 512:(s2_ + 1) * 512],
                                start=(first and s2_ == 0),
                                stop=(last and s2_ == 1))
            b.inc(ins, "ev")
            ngrp += 1
    b.w(ve, "ev")
    ve.tensor_copy(C.rowA[0:1, 0:512], C.pd[0:1, :])
    b.inc(ve.reciprocal(C.rowA[0:1, 0:512], C.rowA[0:1, 0:512]), "v")
    b.w(te, "v")
    b.inc(te.matmul(C.pl[:, 0:512], C.orf[0:1, :], C.rowA[0:1, 0:512],
                    start=True, stop=True), "mm")
    b.w(ve, "mm")
    ve.tensor_copy(C.dinv[:, :], C.pl[:, 0:512])
    b.inc(ve.tensor_mul(
        C.Ofin[:, h2 * T + qi * 512:h2 * T + (qi + 1) * 512],
        C.pO[:, :], C.dinv[:, :]), "ev")
    b.w(te, "ev")
    b.w(sc, "ev")


def emit_attention(b, wq_c, wk_c, wv_c, sub=0):
    from types import SimpleNamespace
    nc = b.nc
    sy, gp, te, ve, sc = nc.sync, nc.gpsimd, nc.tensor, nc.vector, nc.scalar
    t_ = b.ten
    ag_ln, ag_o_in, ag_o = b.hbm["ag_ln"], b.hbm["ag_o_in"], b.hbm["ag_o"]
    C = SimpleNamespace(ag_ln=ag_ln, bqs_pp=t_["bqs_pp"], bk_pp=t_["bk_pp"],
                        bqs_cb=t_["bqs_cb"], bk_cb=t_["bk_cb"],
                        dmask=t_["dmask"], orf=t_["orf"], ocf=t_["ocf"],
                        ocb=t_["ocb"], rowA=t_["rowA"], bvb=t_["bvb"])
    b.bar()
    with (
        nc.sbuf_tensor("Qh", [128, 2 * T], BF16) as Qh,
        nc.sbuf_tensor("Kh", [128, 2 * T], BF16) as Kh,
        nc.sbuf_tensor("Vh", [128, 64 * 256], BF16) as Vh,
        nc.sbuf_tensor("bstat", [1, 8 * 16], F32) as bstat,
        nc.sbuf_tensor("bqbk", [1, 32], F32) as bqbk,
        nc.sbuf_tensor("bkrep", [128, 32], BF16) as bkrep,
        nc.sbuf_tensor("otile", [128, 16], F32) as otile,
        nc.sbuf_tensor("nmhall", [128, 2 * 16], F32) as nmhall,
        nc.sbuf_tensor("mrowb", [1, 6 * 16], F32) as mrowb,
        nc.sbuf_tensor("dinv", [128, 512], F32) as dinv,
    ):
        C.Qh, C.Kh, C.Vh, C.nmhall, C.dinv = Qh, Kh, Vh, nmhall, dinv
        C.st = lambda h2, j: bstat[0:1, (h2 * 4 + j) * 16:
                                   (h2 * 4 + j) * 16 + 16]
        with (
            nc.sbuf_tensor("wqb", [128, DC * 256], BF16) as wqb,
            nc.sbuf_tensor("wkb", [128, DC * 256], BF16) as wkb,
            nc.sbuf_tensor("wvb", [128, DC * 256], BF16) as wvb,
            nc.sbuf_tensor("lns", [128, 2 * DC * 512], BF16) as lns,
            nc.sbuf_tensor("atmp", [128, 512], F32) as atmp,
            nc.psum_tensor("pq", [128, 2 * 512], F32) as pq,
            nc.psum_tensor("pvv", [128, 2 * 512], F32) as pvv,
            nc.psum_tensor("pr1", [1, 512], F32) as pr1,
            nc.psum_tensor("pr2", [1, 512], F32) as pr2,
            nc.psum_tensor("pbc", [128, 256], F32) as pbc,
        ):
            C.wqb, C.wkb, C.wvb, C.lns, C.atmp = wqb, wkb, wvb, lns, atmp
            C.pq, C.pvv, C.pr1, C.pr2 = pq, pvv, pr1, pr2
            b.w(te, "ld")
            b.inc(te.matmul(pbc[:, 0:256], C.orf[0:1, :],
                            t_["bv_row"][0:1, 0:256], start=True, stop=True),
                  "mm")
            b.w(ve, "mm")
            b.inc(ve.tensor_copy(C.bvb[:, :], pbc[:, :]), "v")
            b.inc(ve.memset(otile[:, :], 1.0), "v")
            ve.wait_ge(b.sem["v"], b.cnt["v"])
            for h2 in range(2):
                b.inc(ve.tensor_scalar_mul(bkrep[:, h2 * 16:(h2 + 1) * 16],
                                           otile[:, :],
                                           C.bk_pp[:, h2:h2 + 1]), "v")
            ve.wait_ge(b.sem["v"], b.cnt["v"])
            b.w(te, "v")
            for h2 in range(2):
                ins = te.matmul(pr1[0:1, h2 * 16:(h2 + 1) * 16],
                                C.bqs_cb[:, h2:h2 + 1],
                                bkrep[:, h2 * 16:(h2 + 1) * 16],
                                start=True, stop=True)
            b.inc(ins, "mm")
            b.w(ve, "mm")
            b.inc(ve.tensor_copy(bqbk[0:1, :], pr1[0:1, 0:32]), "v")
            for wsb, wsrc in ((wqb, wq_c), (wkb, wk_c), (wvb, wv_c)):
                b.inc(gp.dma_start(
                    out=v3(wsb[:, :], 16, 256),
                    in_=wsrc.rearrange("(dc p) m -> p dc m", p=128)),
                    "g", 16)
            b.w(sy, "cc")
            import os
            C.base_slab = b.cnt["slab"]
            C.base_ev = b.cnt["ev"]
            C.base_sc = b.cnt["sc"]
            for tb in range(int(os.environ.get("ATTN_TBMAX", "16"))):
                _attn_qkv_tb(b, C, tb)
        b.bar()
        b.w(sc, "v")
        for h2 in range(2):
            sc.activation(C.st(h2, 1)[0:1, :], C.st(h2, 1)[0:1, :], ACT.Sqrt)
            b.inc(sc.activation(C.st(h2, 3)[0:1, :], C.st(h2, 3)[0:1, :],
                                ACT.Sqrt), "sc")
        if sub == 51:
            return
        b.bar()
        with (
            nc.sbuf_tensor("Ofin", [128, 2 * T], BF16) as Ofin,
            nc.sbuf_tensor("pbuf", [128, 2 * 1024], BF16) as pbuf,
            nc.psum_tensor("pl", [128, 2 * 1024], F32) as pl,
            nc.psum_tensor("pO", [128, 512], F32) as pO,
            nc.psum_tensor("pd", [1, 512], F32) as pd,
            nc.psum_tensor("pmh", [128, 16], F32) as pmh,
        ):
            C.Ofin, C.pbuf, C.pl, C.pO, C.pd = Ofin, pbuf, pl, pO, pd
            b.w(ve, "sc")
            for h2 in range(2):
                mr = mrowb[0:1, h2 * 16:(h2 + 1) * 16]
                cg = mrowb[0:1, (2 + h2) * 16:(2 + h2) * 16 + 1]
                kg = mrowb[0:1, (4 + h2) * 16:(4 + h2) * 16 + 1]
                def _vf(ins):
                    b.inc(ins, "v")
                    ve.wait_ge(b.sem["v"], b.cnt["v"])
                _vf(ve.tensor_reduce(cg, C.st(h2, 3)[0:1, :], AX, ALU.max))
                _vf(ve.tensor_reduce(kg, C.st(h2, 2)[0:1, :], AX, ALU.max))
                _vf(ve.tensor_single_scalar(mr, C.st(h2, 1)[0:1, :], cg,
                                            ALU.mult))
                _vf(ve.tensor_add(mr, mr, C.st(h2, 0)[0:1, :]))
                _vf(ve.tensor_single_scalar(mr, mr, kg, ALU.add))
                _vf(ve.tensor_single_scalar(mr, mr,
                                            bqbk[0:1, h2 * 16:h2 * 16 + 1],
                                            ALU.subtract))
                b.inc(ve.tensor_scalar_mul(mr, mr, -1.0), "v")
                ve.wait_ge(b.sem["v"], b.cnt["v"])
                b.w(te, "v")
                b.inc(te.matmul(pmh[:, :], C.orf[0:1, :], mr, start=True,
                                stop=True), "mm")
                b.w(ve, "mm")
                b.inc(ve.tensor_copy(nmhall[:, h2 * 16:(h2 + 1) * 16],
                                     pmh[:, :]), "v")
            for h2 in range(2):
                for qi in range(16):
                    _attn_sweep_qi(b, C, h2, qi)
            if sub == 53:
                b.w(gp, "ev")
                for j in range(8):
                    b.inc(gp.dma_start(out=b.y_dbg[j * 128:(j + 1) * 128, :],
                                       in_=Ofin[:, j * 2048:(j + 1) * 2048]),
                          "g", 16)
                b.w(gp, "g")
                return
            if sub == 54:
                b.w(gp, "ev")
                b.inc(gp.dma_start(out=b.y_dbg[200:328, 0:8],
                                   in_=C.bqs_cb[:, :]), "g", 16)
                b.inc(gp.dma_start(out=b.y_dbg[400:528, 0:8],
                                   in_=C.bk_cb[:, :]), "g", 16)
                b.inc(gp.dma_start(out=b.y_dbg[600:728, 0:8],
                                   in_=C.bqs_pp[:, :]), "g", 16)
                b.inc(gp.dma_start(out=b.y_dbg[0:1, 0:128],
                                   in_=bstat[0:1, :]), "g", 16)
                b.inc(gp.dma_start(out=b.y_dbg[1:2, 0:96],
                                   in_=mrowb[0:1, :]), "g", 16)
                b.inc(gp.dma_start(out=b.y_dbg[2:3, 0:32],
                                   in_=bqbk[0:1, :]), "g", 16)
                b.inc(gp.dma_start(out=b.y_dbg[8:136, 0:32],
                                   in_=nmhall[:, :]), "g", 16)
                b.w(gp, "g")
                return
            if sub == 52:
                return
            b.w(gp, "ev")
            for h2 in range(2):
                b.inc(gp.dma_start(out=ag_o_in[h2 * 128:(h2 + 1) * 128, :],
                                   in_=Ofin[:, h2 * T:(h2 + 1) * T]),
                      "g", 16)
            b.w(gp, "g")
            b.inc(gp.collective_compute(
                "AllGather", ALU.bypass, ins=[ag_o_in[:, :]],
                outs=[ag_o[:, :]], replica_groups=[list(range(8))]), "cc")
            b.w(gp, "cc")


def emit_outproj(b, wo_in, xT2):
    nc = b.nc
    sy, gp, te, ve, sc = nc.sync, nc.gpsimd, nc.tensor, nc.vector, nc.scalar
    ag_o, xsp = b.hbm["ag_o"], b.hbm["xsp"]
    bo_pp = b.ten["bo_pp"]
    pid = nc.sync.partition_id()
    tok0 = pid * TPC
    b.bar()
    with (
        nc.sbuf_tensor("oall", [128, 16 * TPC], BF16) as oall,
        nc.sbuf_tensor("wos2", [128, 2 * DC * 128], BF16) as wos2,
        nc.sbuf_tensor("ope", [128, TPC], F32) as ope,
        nc.psum_tensor("po2", [128, 2 * TPC], F32) as po2,
    ):
        b.w(sy, "cc")
        b.w(sy, "g")
        b.inc(sy.dma_start(out=xT2[:, :], in_=xsp[:, :]), "x2", 16)
        for hc in range(16):
            rank, loc = hc // 2, hc % 2
            b.inc(sy.dma_start(
                out=oall[:, hc * TPC:(hc + 1) * TPC],
                in_=ag_o[rank * 256 + loc * 128:rank * 256 + (loc + 1) * 128,
                         bass.ds(tok0, TPC)]), "x1", 16)
        base_x1 = b.cnt["x1"]
        base_g = b.cnt["g"]
        base_ps = b.cnt["ps"]
        base_ev = b.cnt["ev"]
        for dmc in range(DC):
            wsl = wos2[:, (dmc % 2) * DC * 128:(dmc % 2 + 1) * DC * 128]
            if dmc >= 2:
                gp.wait_ge(b.sem["ps"], base_ps + dmc - 1)
            b.inc(gp.dma_start(
                out=v3(wsl[:, :], 16, 128),
                in_=wo_in.rearrange("(hc p) m -> p hc m", p=128)[
                    :, :, dmc * 128:(dmc + 1) * 128]), "g", 16)
            te.wait_ge(b.sem["x1"], base_x1)
            te.wait_ge(b.sem["g"], base_g + 16 * (dmc + 1))
            if dmc >= 2:
                te.wait_ge(b.sem["ev"], base_ev + dmc - 1)
            ppw = dmc % 2
            pmw = po2[:, ppw * TPC:(ppw + 1) * TPC]
            for th in range(2):
                for hc in range(16):
                    ins = te.matmul(
                        pmw[:, th * 512:(th + 1) * 512],
                        wsl[:, hc * 128:(hc + 1) * 128],
                        oall[:, hc * TPC + th * 512:hc * TPC + (th + 1) * 512],
                        start=(hc == 0), stop=(hc == 15))
            b.inc(ins, "ps")
            sc.wait_ge(b.sem["ps"], base_ps + dmc + 1)
            b.inc(sc.activation(ope[:, :], pmw[:, :], ACT.Identity,
                                bias=bo_pp[:, dmc:dmc + 1]), "sc")
            ve.wait_ge(b.sem["sc"], b.cnt["sc"])
            ve.wait_ge(b.sem["x2"], b.cnt["x2"])
            b.inc(ve.tensor_add(xT2[:, dmc * TPC:(dmc + 1) * TPC],
                                xT2[:, dmc * TPC:(dmc + 1) * TPC],
                                ope[:, :]), "ev")
        b.w(te, "ev")


def emit_store_nat(b, xT, y):
    nc = b.nc
    sy, te, ve = nc.sync, nc.tensor, nc.vector
    idf = b.ten["idf"]
    b.bar()
    with (
        nc.sbuf_tensor("ostA", [128, 2048], F32) as ostA,
        nc.sbuf_tensor("ostB", [128, 2048], F32) as ostB,
        nc.psum_tensor("pst", [128, 512], F32) as pt,
    ):
        obufs = [ostA, ostB]
        base_pp = [b.cnt["stA"], b.cnt["stB"]]
        for tt in range(8):
            buf = obufs[tt % 2]
            stp = ["stA", "stB"][tt % 2]
            for grp in range(4):
                b.w(te, "ev")
                if tt >= 2 and grp == 0:
                    te.wait_ge(b.sem[stp], base_pp[tt % 2] + 16 * (tt // 2))
                for j in range(4):
                    dc = grp * 4 + j
                    ins = te.transpose(
                        pt[:, j * 128:(j + 1) * 128],
                        xT[:, dc * TPC + tt * 128:dc * TPC + (tt + 1) * 128],
                        idf[:])
                b.inc(ins, "mm")
                b.w(ve, "mm")
                for j in range(4):
                    dc = grp * 4 + j
                    ins = ve.tensor_copy(buf[:, dc * 128:(dc + 1) * 128],
                                         pt[:, j * 128:(j + 1) * 128])
                b.inc(ins, "ev")
            b.w(sy, "ev")
            b.inc(sy.dma_start(out=y[tt * 128:(tt + 1) * 128, :],
                               in_=buf[:, :]), stp, 16)


from concourse.bass_utils import run_bass_kernel_spmd


def _shard_inputs(inputs):
    x = np.ascontiguousarray(inputs["L0_attn_query"][0], dtype=np.float32)
    w_gv = np.ascontiguousarray(inputs["L0_ffn_w_gv"], dtype=np.float32)
    w_out = np.ascontiguousarray(inputs["L0_ffn_w_out"], dtype=np.float32)
    nw = np.asarray(inputs["L0_ffn_norm_w"], np.float32).reshape(1, 2048)
    wq = np.asarray(inputs["L0_attn_w_q"], np.float32)
    wk = np.asarray(inputs["L0_attn_w_k"], np.float32)
    wv = np.asarray(inputs["L0_attn_w_v"], np.float32)
    wo = np.ascontiguousarray(inputs["L0_attn_w_o"], dtype=np.float32)
    bq = np.asarray(inputs["L0_attn_b_q"], np.float32).reshape(1, 2048)
    bk = np.asarray(inputs["L0_attn_b_k"], np.float32).reshape(1, 2048)
    bv = np.asarray(inputs["L0_attn_b_v"], np.float32).reshape(1, 2048)
    bo = np.asarray(inputs["L0_attn_b_o"], np.float32).reshape(1, 2048)
    maps = []
    for c in range(8):
        maps.append(dict(
            x_c=np.ascontiguousarray(x[c * 1024:(c + 1) * 1024]),
            wgv_c=np.ascontiguousarray(w_gv[c * 2048:(c + 1) * 2048]),
            wout_c=np.ascontiguousarray(w_out[c * 256:(c + 1) * 256]),
            nw_in=nw,
            wq_c=np.ascontiguousarray(wq[:, c * 256:(c + 1) * 256]),
            wk_c=np.ascontiguousarray(wk[:, c * 256:(c + 1) * 256]),
            wv_c=np.ascontiguousarray(wv[:, c * 256:(c + 1) * 256]),
            wo_in=wo,
            bq_c=np.ascontiguousarray(bq[:, c * 256:(c + 1) * 256]),
            bk_c=np.ascontiguousarray(bk[:, c * 256:(c + 1) * 256]),
            bv_c=np.ascontiguousarray(bv[:, c * 256:(c + 1) * 256]),
            bo_in=bo,
        ))
    return maps


def kernel(**inputs):
    nc = build(stage=3, sub=0)
    maps = _shard_inputs(inputs)
    res = run_bass_kernel_spmd(nc, maps, core_ids=list(range(8)))
    out = np.concatenate([res.results[c]["y"] for c in range(8)], axis=0)
    return out.astype(np.float32, copy=False)[None]


# version 02 (final)
# speedup vs baseline: 1.0074x; 1.0074x over previous
import numpy as np
import ml_dtypes
import concourse.bass as bass
import concourse.mybir as mybir

F32 = mybir.dt.float32
BF16 = mybir.dt.bfloat16
AX = mybir.AxisListType.X
ALU = mybir.AluOpType
ACT = mybir.ActivationFunctionType

MAGIC = float(np.float32(1.5 * 2 ** 23))
RSQ = float(np.float32(1.0) / np.sqrt(np.float32(128.0)))
D = 2048
DC = 16
TPC = 1024
HID = 8192
OC = 64
T = 8192
WGV_N = 16384 * 2048
WOUT_N = 2048 * 8192


class B:
    def __init__(self, nc):
        self.nc = nc
        self.sem = {}
        self.cnt = {}

    def mksem(self, *names):
        for n in names:
            self.sem[n] = self.nc.semaphore(n).__enter__()
            self.cnt[n] = 0

    def inc(self, instr, name, k=1):
        instr.then_inc(self.sem[name], k)
        self.cnt[name] += k
        return instr

    def w(self, eng, name, n=None):
        eng.wait_ge(self.sem[name], self.cnt[name] if n is None else n)

    def bar(self):
        self.nc.all_engine_barrier()


def v3(ap2d, n, m):
    return ap2d.rearrange("p (a c) -> p a c", a=n)


def build(stage=3, sub=0):
    nc = bass.Bass("TRN2", target_bir_lowering=False)
    b = B(nc)

    x_c = nc.dram_tensor("x_c", [TPC, D], F32, kind="ExternalInput").ap()
    wgv_c = nc.dram_tensor("wgv_c", [2048, 2048], F32, kind="ExternalInput").ap()
    wout_c = nc.dram_tensor("wout_c", [256, 8192], F32, kind="ExternalInput").ap()
    nw_in = nc.dram_tensor("nw_in", [1, D], F32, kind="ExternalInput").ap()
    wq_c = nc.dram_tensor("wq_c", [D, 256], F32, kind="ExternalInput").ap()
    wk_c = nc.dram_tensor("wk_c", [D, 256], F32, kind="ExternalInput").ap()
    wv_c = nc.dram_tensor("wv_c", [D, 256], F32, kind="ExternalInput").ap()
    wo_in = nc.dram_tensor("wo_in", [D, D], F32, kind="ExternalInput").ap()
    bq_c = nc.dram_tensor("bq_c", [1, 256], F32, kind="ExternalInput").ap()
    bk_c = nc.dram_tensor("bk_c", [1, 256], F32, kind="ExternalInput").ap()
    bv_c = nc.dram_tensor("bv_c", [1, 256], F32, kind="ExternalInput").ap()
    bo_in = nc.dram_tensor("bo_in", [1, D], F32, kind="ExternalInput").ap()
    y = nc.dram_tensor("y", [TPC, D], F32, kind="ExternalOutput").ap()

    u_hbm = nc.dram_tensor("u_hbm", [OC, 128, TPC], F32, kind="Internal").ap()
    k2_hbm = nc.dram_tensor("k2_hbm", [OC, 128, TPC], BF16, kind="Internal").ap()
    xsp = nc.dram_tensor("xsp", [128, DC * TPC], F32, kind="Internal").ap()
    ag_wgv_in = nc.dram_tensor("ag_wgv_in", [2048, 2048], BF16, kind="Internal").ap()
    ag_wgv = nc.dram_tensor("ag_wgv", [8 * 2048, 2048], BF16, kind="Internal",
                            addr_space="Shared").ap()
    ag_wout_in = nc.dram_tensor("ag_wout_in", [8192, 256], BF16,
                                kind="Internal").ap()
    ag_wout = nc.dram_tensor("ag_wout", [8 * 8192, 256], BF16, kind="Internal",
                             addr_space="Shared").ap()
    ar_s_in = nc.dram_tensor("ar_s_in", [1, 2], F32, kind="Internal").ap()
    ar_s = nc.dram_tensor("ar_s", [1, 2], F32, kind="Internal").ap()
    ag_ln_in = nc.dram_tensor("ag_ln_in", [2048, TPC], BF16, kind="Internal").ap()
    ag_ln = nc.dram_tensor("ag_ln", [8 * 2048, TPC], BF16, kind="Internal",
                           addr_space="Shared").ap()
    ag_o_in = nc.dram_tensor("ag_o_in", [256, T], BF16, kind="Internal").ap()
    ag_o = nc.dram_tensor("ag_o", [8 * 256, T], BF16, kind="Internal",
                          addr_space="Shared").ap()

    h_idf = nc.inline_tensor(np.eye(128, dtype=np.float32), name="h_idf").ap()
    h_idb = nc.inline_tensor(np.eye(128, dtype=ml_dtypes.bfloat16),
                             name="h_idb").ap()
    h_ocf = nc.inline_tensor(np.ones((128, 1), np.float32), name="h_ocf").ap()
    h_ocb = nc.inline_tensor(np.ones((128, 1), ml_dtypes.bfloat16),
                             name="h_ocb").ap()
    h_orf = nc.inline_tensor(np.ones((1, 128), np.float32), name="h_orf").ap()
    _dm = np.zeros((128, 4 * 512), np.float32)
    for _s in range(4):
        _p = np.arange(128)[:, None]
        _j = np.arange(512)[None, :]
        _dm[:, _s * 512:(_s + 1) * 512] = np.where(
            _j - _p - 128 * _s >= 0, 0.0, np.float32(-1e10))
    h_dmask = nc.inline_tensor(_dm, name="h_dmask").ap()

    b.mksem("ld", "g", "mm", "v", "sc", "cc", "st", "slab", "free", "ps", "ev",
            "x1", "x2", "aux", "ug", "k2s", "tq", "stA", "stB", "ugA", "ugB",
            "k2A", "k2B")

    from contextlib import ExitStack
    with ExitStack() as es:
        S = lambda name, shape, dt: es.enter_context(nc.sbuf_tensor(name, shape, dt))
        idf = S("idf", [128, 128], F32)
        idb = S("idb", [128, 128], BF16)
        ocf = S("c_ocf", [128, 1], F32)
        ocb = S("c_ocb", [128, 1], BF16)
        orf = S("c_orf", [1, 128], F32)
        nw_pp = S("nw_pp", [128, DC], F32)
        bo_pp = S("bo_pp", [128, DC], F32)
        bq_pp = S("bq_pp", [128, 8], F32)
        bk_pp = S("bk_pp", [128, 8], F32)
        bqs_pp = S("bqs_pp", [128, 8], F32)
        bqs_cb = S("bqs_cb", [128, 8], BF16)
        bk_cb = S("bk_cb", [128, 8], BF16)
        bv_row = S("bv_row", [1, 256], F32)
        bvb = S("bvb", [128, 256], F32)
        ceps = S("ceps", [1, 1], F32)
        rowA = S("rowA", [1, TPC], F32)
        rowB = S("rowB", [1, TPC], F32)
        rowC = S("rowC", [1, TPC], F32)
        rowD = S("rowD", [1, TPC], F32)
        srow = S("srow", [1, 16], F32)
        scol = S("scol", [128, 16], F32)
        mbt = S("mbt", [128, TPC], F32)
        rbt = S("rbt", [128, TPC], F32)
        fbt = S("fbt", [128, TPC], F32)
        cbt = S("cbt", [128, TPC], F32)
        dmask = S("dmask", [128, 4 * 512], F32)
        gmc = S("gmc", [128, 8], F32)
        zerot = S("zerot", [128, 128], BF16)
        b.ten = dict(idf=idf, idb=idb, ocf=ocf, ocb=ocb, orf=orf, nw_pp=nw_pp,
                     bo_pp=bo_pp, bq_pp=bq_pp, bk_pp=bk_pp, bqs_pp=bqs_pp,
                     bqs_cb=bqs_cb, bk_cb=bk_cb, bv_row=bv_row, bvb=bvb,
                     ceps=ceps, rowA=rowA, rowB=rowB, rowC=rowC, rowD=rowD,
                     srow=srow, scol=scol, mbt=mbt, rbt=rbt, fbt=fbt, cbt=cbt,
                     dmask=dmask, gmc=gmc, zerot=zerot)
        b.hbm = dict(u_hbm=u_hbm, k2_hbm=k2_hbm, xsp=xsp, ag_wgv=ag_wgv,
                     ag_wout=ag_wout, ag_ln_in=ag_ln_in, ag_ln=ag_ln,
                     ag_o_in=ag_o_in, ag_o=ag_o)

        with nc.Block() as _blk:
            sy, gp, te, ve, sc = nc.sync, nc.gpsimd, nc.tensor, nc.vector, nc.scalar
            for dst, src in ((idf, h_idf), (idb, h_idb), (ocf, h_ocf),
                             (ocb, h_ocb), (orf, h_orf), (bv_row, bv_c)):
                b.inc(sy.dma_start(out=dst[:, :], in_=src[:, :]), "ld", 16)
            with nc.allow_non_contiguous_dma(reason="tiny per-partition vecs"):
                for dst, src in ((nw_pp, nw_in), (bo_pp, bo_in)):
                    b.inc(sy.dma_start(
                        out=dst[:, :],
                        in_=src.rearrange("o (dc p) -> (o p) dc", p=128)),
                        "ld", 16)
                for dst, src in ((bq_pp, bq_c), (bk_pp, bk_c)):
                    b.inc(sy.dma_start(
                        out=dst[:, 0:2],
                        in_=src.rearrange("o (h p) -> (o p) h", p=128)),
                        "ld", 16)
            b.w(ve, "ld")
            ve.memset(zerot[:, :], 0.0)
            ve.memset(ceps[:, :], 1e-5)
            b.inc(ve.tensor_scalar_mul(bqs_pp[:, 0:8], bq_pp[:, 0:8], RSQ),
                  "v")
            ve.wait_ge(b.sem["v"], b.cnt["v"])
            ve.tensor_copy(bqs_cb[:, 0:8], bqs_pp[:, 0:8])
            b.inc(ve.tensor_copy(bk_cb[:, 0:8], bk_pp[:, 0:8]), "v")
            b.inc(gp.dma_start(out=dmask[:, :], in_=h_dmask[:, :]), "g", 16)

            b.dump_srow = (sub == 24)
            b.dump_y = y
            emit_prepass(b, wgv_c, wout_c, ag_wgv_in, ag_wgv, ag_wout_in,
                         ag_wout, ar_s_in, ar_s)
            if sub == 24:
                b.w(gp, "stA")
                return nc

            if sub == 25:
                b.w(gp, "cc", 3)
                b.inc(gp.dma_start(out=y[:, :], in_=ag_wgv[0:1024, :]),
                      "stA", 16)
                b.w(gp, "stA")
                return nc
            if sub == 26:
                b.w(gp, "cc", 3)
                b.inc(gp.dma_start(out=y.rearrange("a (c d) -> (a c) d", c=8),
                                   in_=ag_wout[0:8192, :]), "stA", 16)
                b.w(gp, "stA")
                return nc
            b.y_dbg = y
            with nc.sbuf_tensor("xT", [128, DC * TPC], F32) as xT:
                b.bar()
                emit_xload(b, x_c, xT)
                b.bar()
                if sub == 9:
                    emit_store_nat(b, xT, y)
                    b.w(gp, "stA")
                    b.w(gp, "stB")
                    return nc
                emit_ffn(b, xT, tag="f1", sub=sub)
                b.bar()
                if sub in (31, 32, 33, 35):
                    return nc
                if sub in (41, 42) and sub == 42:
                    emit_store_nat(b, xT, y)
                    b.w(nc.gpsimd, "stA")
                    b.w(nc.gpsimd, "stB")
                    return nc
                if sub == 41:
                    emit_store_nat(b, xT, y)
                    b.w(nc.gpsimd, "stA")
                    b.w(nc.gpsimd, "stB")
                    return nc
                if stage == 1:
                    emit_store_nat(b, xT, y)
                    b.w(gp, "stA")
                    b.w(gp, "stB")
                    return nc
                emit_ln_stage(b, xT)
                b.bar()
            if sub == 50:
                with nc.sbuf_tensor("xTr", [128, DC * TPC], F32) as xTr:
                    b.w(sy, "cc")
                    b.inc(sy.dma_start(out=xTr[:, :],
                                       in_=b.hbm["xsp"][:, :]), "x2", 16)
                    b.w(te, "x2")
                    b.bar()
                    emit_store_nat(b, xTr, y)
                    b.w(gp, "stA")
                    b.w(gp, "stB")
                return nc
            emit_attention(b, wq_c, wk_c, wv_c, sub=sub)
            b.bar()
            if sub in (53, 54):
                return nc
            if sub in (51, 52):
                with nc.sbuf_tensor("xTr", [128, DC * TPC], F32) as xTr:
                    b.inc(sy.dma_start(out=xTr[:, :],
                                       in_=b.hbm["xsp"][:, :]), "x2", 16)
                    b.w(te, "x2")
                    b.bar()
                    emit_store_nat(b, xTr, y)
                    b.w(gp, "stA")
                    b.w(gp, "stB")
                return nc
            with nc.sbuf_tensor("xT2", [128, DC * TPC], F32) as xT2:
                emit_outproj(b, wo_in, xT2)
                b.bar()
                if stage == 2:
                    emit_store_nat(b, xT2, y)
                    b.w(gp, "stA")
                    b.w(gp, "stB")
                    return nc
                emit_ffn(b, xT2, tag="f2")
                b.bar()
                emit_store_nat(b, xT2, y)
                b.w(gp, "stA")
                b.w(gp, "stB")
    return nc


def emit_prepass(b, wgv_c, wout_c, ag_wgv_in, ag_wgv, ag_wout_in, ag_wout,
                 ar_s_in, ar_s):
    nc = b.nc
    sy, gp, te, ve, sc = nc.sync, nc.gpsimd, nc.tensor, nc.vector, nc.scalar
    t_ = b.ten
    srow, scol, ocf, orf, idb = (t_["srow"], t_["scol"], t_["ocf"], t_["orf"],
                                 t_["idb"])
    rowA, rowB = t_["rowA"], t_["rowB"]
    with (
        nc.sbuf_tensor("pwA", [128, 2048], F32) as pwA,
        nc.sbuf_tensor("pwB", [128, 2048], F32) as pwB,
        nc.sbuf_tensor("pw_red24", [128, 32], F32) as pw_red24,
        nc.sbuf_tensor("pw_q", [128, 2048], BF16) as pw_q,
        nc.sbuf_tensor("pw_qT", [128, 2048], BF16) as pw_qT,
        nc.psum_tensor("pp_sum", [1, 32], F32) as pp_sum,
        nc.psum_tensor("pp_t", [128, 512], BF16) as pp_t,
        nc.psum_tensor("pp_c", [128, 8], F32) as pp_c,
        nc.psum_tensor("pp_sr", [1, 128], F32) as pp_sr,
    ):
        bufs = [pwA, pwB]
        b.w(te, "ld")
        ve.memset(pw_red24[:, :], 0.0)
        base_x1 = b.cnt["x1"]
        base_free = b.cnt["free"]
        for i in range(24):
            buf = bufs[i % 2]
            if i >= 2:
                b.w(sy, "free", base_free + i - 1)
            if i < 16:
                src = wgv_c[i * 128:(i + 1) * 128, :]
            else:
                wb = i - 16
                pb, og = wb // 4, wb % 4
                src = wout_c[pb * 128:(pb + 1) * 128,
                             og * 2048:(og + 1) * 2048]
            b.inc(sy.dma_start(out=buf[:, :], in_=src), "x1", 16)
            b.w(ve, "x1", base_x1 + 16 * (i + 1))
            b.inc(ve.tensor_reduce(pw_red24[:, i:i + 1], buf[:, :], AX,
                                   ALU.add, apply_absolute_value=True),
                  "free")
        b.w(te, "free")
        b.inc(te.matmul(pp_sum[0:1, :], ocf[:, :], pw_red24[:, :], start=True,
                        stop=True), "mm")
        b.w(ve, "mm")
        b.inc(ve.tensor_copy(rowA[0:1, 0:32], pp_sum[0:1, :]), "v")
        ve.tensor_reduce(srow[0:1, 0:1], rowA[0:1, 0:16], AX, ALU.add)
        b.inc(ve.tensor_reduce(srow[0:1, 1:2], rowA[0:1, 16:24], AX, ALU.add),
              "v")
        b.w(gp, "v")
        b.inc(gp.dma_start(out=ar_s_in[:, :], in_=srow[0:1, 0:2]), "g", 16)
        b.w(gp, "g")
        b.inc(gp.collective_compute("AllReduce", ALU.add, ins=[ar_s_in[:, :]],
                                    outs=[ar_s[:, :]],
                                    replica_groups=[list(range(8))]), "cc")
        b.w(gp, "cc")
        b.inc(gp.dma_start(out=srow[0:1, 2:4], in_=ar_s[:, :]), "g", 16)
        b.w(ve, "g")
        ve.tensor_scalar(srow[0:1, 4:5], srow[0:1, 2:3], 1.0 / WGV_N, 1e-5,
                         ALU.mult, ALU.max)
        b.inc(ve.tensor_scalar(srow[0:1, 5:6], srow[0:1, 3:4], 1.0 / WOUT_N,
                               1e-5, ALU.mult, ALU.max), "v")
        ve.memset(rowA[0:1, 0:128], 1.0)
        ve.tensor_scalar_mul(rowA[0:1, 0:64], rowA[0:1, 0:64], srow[0:1, 4:5])
        ve.tensor_scalar_mul(rowA[0:1, 64:128], rowA[0:1, 64:128],
                             srow[0:1, 5:6])
        b.inc(ve.reciprocal(rowB[0:1, 0:128], rowA[0:1, 0:128]), "v")
        b.w(te, "v")
        b.inc(te.matmul(pp_c[:, 0:8], orf[0:1, :], rowB[0:1, 0:8],
                        start=True, stop=True), "mm")
        b.w(ve, "mm")
        b.inc(ve.tensor_copy(scol[:, 0:8], pp_c[:, 0:8]), "v")
        b.w(te, "v")
        b.inc(te.matmul(pp_c[:, 0:8], orf[0:1, :], rowB[0:1, 64:72],
                        start=True, stop=True), "mm")
        b.w(ve, "mm")
        b.inc(ve.tensor_copy(scol[:, 8:16], pp_c[:, 0:8]), "v")

        if getattr(b, "dump_srow", False):
            b.w(gp, "v")
            b.inc(gp.dma_start(out=b.dump_y[0:1, 0:16], in_=srow[0:1, :]),
                  "stA", 16)
        b.bar()
        def one_block(src_ap, invcol, dst_ap):
            buf = bufs[one_block.i % 2]
            if one_block.i >= 2:
                b.w(sy, "k2s", b.cnt["k2s"] - 1)
            b.inc(sy.dma_start(out=buf[:, :], in_=src_ap), "x1", 16)
            b.w(ve, "x1")
            b.w(ve, "tq")
            ve.tensor_scalar_mul(buf[:, :], buf[:, :], invcol)
            ve.tensor_scalar(buf[:, :], buf[:, :], MAGIC, MAGIC, ALU.add,
                             ALU.subtract)
            b.inc(ve.tensor_scalar(pw_q[:, :], buf[:, :], 1.0, -1.0,
                                   ALU.min, ALU.max), "k2s")
            b.w(te, "k2s")
            b.w(te, "g")
            for grp in range(4):
                b.w(te, "ev")
                for j in range(4):
                    dcx = grp * 4 + j
                    ins = te.transpose(pp_t[:, j * 128:(j + 1) * 128],
                                       pw_q[:, dcx * 128:(dcx + 1) * 128],
                                       idb[:])
                b.inc(ins, "mm")
                if grp == 3:
                    b.inc(te.nop(), "tq")
                b.w(ve, "mm")
                b.inc(ve.tensor_copy(pw_qT[:, grp * 512:(grp + 1) * 512],
                                     pp_t[:, :]), "ev")
            b.w(gp, "ev")
            b.inc(gp.dma_start(out=dst_ap, in_=v3(pw_qT[:, :], 16, 128)),
                  "g", 16)
            one_block.i += 1
        one_block.i = 0

        for ob in range(16):
            dst = ag_wgv_in.rearrange("(dc p) o -> p dc o", p=128)[
                :, :, ob * 128:(ob + 1) * 128]
            one_block(wgv_c[ob * 128:(ob + 1) * 128, :], scol[:, 0:1], dst)
        for pb in range(2):
            for og in range(4):
                dst = ag_wout_in.rearrange("(og oc p) m -> p og oc m",
                                           og=4, oc=16)[
                    :, og, :, pb * 128:(pb + 1) * 128]
                one_block(wout_c[pb * 128:(pb + 1) * 128,
                                 og * 2048:(og + 1) * 2048], scol[:, 8:9], dst)
        b.w(gp, "g")
        b.inc(gp.collective_compute("AllGather", ALU.bypass,
                                    ins=[ag_wgv_in[:, :]], outs=[ag_wgv[:, :]],
                                    replica_groups=[list(range(8))]), "cc")
        b.inc(gp.collective_compute("AllGather", ALU.bypass,
                                    ins=[ag_wout_in[:, :]], outs=[ag_wout[:, :]],
                                    replica_groups=[list(range(8))]), "cc")
    b.bar()


def emit_xload(b, x_c, xT):
    nc = b.nc
    sy, te, ve = nc.sync, nc.tensor, nc.vector
    idf = b.ten["idf"]
    with (
        nc.sbuf_tensor("xlA", [128, 2048], F32) as xlA,
        nc.sbuf_tensor("xlB", [128, 2048], F32) as xlB,
        nc.psum_tensor("px_t", [128, 512], F32) as pt,
    ):
        bufs = [xlA, xlB]
        base_x1 = b.cnt["x1"]
        base_free = b.cnt["free"]
        for tt in range(8):
            buf = bufs[tt % 2]
            b.w(sy, "free")
            b.inc(sy.dma_start(out=buf[:, :],
                               in_=x_c[tt * 128:(tt + 1) * 128, :]), "x1", 16)
            b.w(te, "x1")
            for grp in range(4):
                b.w(te, "ev")
                for j in range(4):
                    dc = grp * 4 + j
                    ins = te.transpose(pt[:, j * 128:(j + 1) * 128],
                                       buf[:, dc * 128:(dc + 1) * 128], idf[:])
                b.inc(ins, "mm")
                if grp == 3:
                    b.inc(te.nop(), "free")
                b.w(ve, "mm")
                for j in range(4):
                    dc = grp * 4 + j
                    ins = ve.tensor_copy(
                        xT[:, dc * TPC + tt * 128:dc * TPC + (tt + 1) * 128],
                        pt[:, j * 128:(j + 1) * 128])
                b.inc(ins, "ev")


def emit_stats(b, xT, mean_row, rstd_row, tag):
    nc = b.nc
    te, ve, sc = nc.tensor, nc.vector, nc.scalar
    ocf, ceps, rowC = b.ten["ocf"], b.ten["ceps"], b.ten["rowC"]
    with (
        nc.sbuf_tensor(f"sq_{tag}", [128, 2 * TPC], F32) as sqt,
        nc.psum_tensor(f"psx_{tag}", [1, TPC], F32) as psx,
        nc.psum_tensor(f"psq_{tag}", [1, TPC], F32) as psq,
    ):
        b.w(sc, "ev")
        b.w(te, "ev")
        b.w(ve, "ev")
        base_mm = b.cnt["mm"]
        for dc in range(DC):
            xdc = xT[:, dc * TPC:(dc + 1) * TPC]
            sq = sqt[:, (dc % 2) * TPC:(dc % 2 + 1) * TPC]
            if dc >= 2:
                sc.wait_ge(b.sem["mm"], base_mm + dc - 1)
            b.inc(sc.activation(sq[:, :], xdc, ACT.Square), "sc")
            for th in range(2):
                te.matmul(psx[0:1, th * 512:(th + 1) * 512], ocf[:, :],
                          xdc[:, th * 512:(th + 1) * 512],
                          start=(dc == 0), stop=(dc == DC - 1))
            b.w(te, "sc")
            for th in range(2):
                ins = te.matmul(psq[0:1, th * 512:(th + 1) * 512], ocf[:, :],
                                sq[:, th * 512:(th + 1) * 512],
                                start=(dc == 0), stop=(dc == DC - 1))
            b.inc(ins, "mm")
        b.w(ve, "mm")
        ve.tensor_scalar_mul(mean_row[0:1, :], psx[0:1, :], 1.0 / D)
        ve.tensor_scalar_mul(rstd_row[0:1, :], psq[0:1, :], 1.0 / D)
        ve.tensor_mul(rowC[0:1, :], mean_row[0:1, :], mean_row[0:1, :])
        b.inc(ve.tensor_sub(rstd_row[0:1, :], rstd_row[0:1, :], rowC[0:1, :]),
              "v")
        b.w(sc, "v")
        b.inc(sc.activation(rstd_row[0:1, :], rstd_row[0:1, :], ACT.Sqrt,
                            bias=ceps[0:1, :]), "sc")
        b.w(ve, "sc")
        b.inc(ve.reciprocal(rstd_row[0:1, :], rstd_row[0:1, :]), "v")


def bcast_row(b, row_slice, dst_tile, psb):
    nc = b.nc
    te, ve = nc.tensor, nc.vector
    orf = b.ten["orf"]
    n = row_slice.shape[1]
    nt = (n + 511) // 512
    for th in range(nt):
        w_ = min(512, n - th * 512)
        b.w(te, "v")
        b.inc(te.matmul(psb[:, 0:w_], orf[0:1, :],
                        row_slice[0:1, th * 512:th * 512 + w_],
                        start=True, stop=True), "mm")
        b.w(ve, "mm")
        b.inc(ve.tensor_copy(dst_tile[:, th * 512:th * 512 + w_],
                             psb[:, 0:w_]), "v")


def rowify(b, acc, out_row, psT, psR):
    nc = b.nc
    te, ve = nc.tensor, nc.vector
    idf, gmc = b.ten["idf"], b.ten["gmc"]
    for j in range(8):
        b.w(te, "v")
        b.inc(te.transpose(psT[:, (j % 4) * 128:(j % 4 + 1) * 128],
                           acc[:, j * 128:(j + 1) * 128], idf[:]), "mm")
        b.w(ve, "mm")
        b.inc(ve.tensor_reduce(gmc[:, j:j + 1],
                               psT[:, (j % 4) * 128:(j % 4 + 1) * 128],
                               AX, ALU.max), "v")
    b.w(te, "v")
    for j in range(8):
        ins = te.transpose(psR[0:1, j * 128:(j + 1) * 128], gmc[:, j:j + 1],
                           idf[:])
    b.inc(ins, "mm")
    b.w(ve, "mm")
    b.inc(ve.tensor_copy(out_row[0:1, :], psR[0:1, :]), "v")


def emit_ffn(b, xT, tag, sub=0):
    b.sub = sub
    import os
    b.sub42_slot = int(os.environ.get("SUB42_SLOT", "0"))
    b.ffn_abort = False
    nc = b.nc
    sy, gp, te, ve, sc = nc.sync, nc.gpsimd, nc.tensor, nc.vector, nc.scalar
    t_ = b.ten
    rowA, rowB, rowC, rowD = t_["rowA"], t_["rowB"], t_["rowC"], t_["rowD"]
    mbt, rbt, fbt, cbt = t_["mbt"], t_["rbt"], t_["fbt"], t_["cbt"]
    nw_pp, srow, ceps, ocf = t_["nw_pp"], t_["srow"], t_["ceps"], t_["ocf"]
    u_hbm, k2_hbm = b.hbm["u_hbm"], b.hbm["k2_hbm"]
    ag_wgv, ag_wout = b.hbm["ag_wgv"], b.hbm["ag_wout"]

    with nc.sbuf_tensor(f"k1_{tag}", [128, DC * TPC], BF16) as k1:
        b.bar()
        emit_stats(b, xT, rowA, rowB, tag)
        b.bar()
        with nc.psum_tensor(f"psb_{tag}", [128, 512], F32) as psb:
            bcast_row(b, rowA[0:1, :], mbt, psb)
            bcast_row(b, rowB[0:1, :], rbt, psb)
        b.bar()
        with (
            nc.sbuf_tensor(f"ln0_{tag}", [128, 2 * TPC], F32) as ln0t,
            nc.sbuf_tensor(f"gt_{tag}", [128, TPC], F32) as gt,
            nc.sbuf_tensor(f"sq1_{tag}", [128, 2 * TPC], F32) as sqt,
            nc.sbuf_tensor(f"acc_{tag}", [128, TPC], F32) as acc,
            nc.psum_tensor(f"psq2_{tag}", [1, TPC], F32) as psq2,
        ):
            ve.memset(acc[:, :], 0.0)
            base_mm = b.cnt["mm"]
            base_sc = b.cnt["sc"]
            for dc in range(DC):
                xdc = xT[:, dc * TPC:(dc + 1) * TPC]
                ln0 = ln0t[:, (dc % 2) * TPC:(dc % 2 + 1) * TPC]
                sq = sqt[:, (dc % 2) * TPC:(dc % 2 + 1) * TPC]
                if dc >= 2:
                    ve.wait_ge(b.sem["sc"], base_sc + dc - 1)
                ve.tensor_sub(ln0[:, :], xdc, mbt[:, :])
                ve.tensor_mul(ln0[:, :], ln0[:, :], rbt[:, :])
                ve.tensor_scalar_mul(gt[:, :], ln0[:, :], nw_pp[:, dc:dc + 1])
                ve.tensor_max(acc[:, :], acc[:, :], gt[:, :])
                ve.scalar_tensor_tensor(acc[:, :], gt[:, :], -1.0, acc[:, :],
                                        ALU.mult, ALU.max)
                b.inc(ve.scalar_tensor_tensor(xdc, ln0[:, :], 0.5, xdc,
                                              ALU.mult, ALU.add), "v")
                sc.wait_ge(b.sem["v"], b.cnt["v"])
                b.inc(sc.activation(sq[:, :], ln0[:, :], ACT.Square), "sc")
                te.wait_ge(b.sem["sc"], base_sc + dc + 1)
                for th in range(2):
                    ins = te.matmul(psq2[0:1, th * 512:(th + 1) * 512],
                                    ocf[:, :], sq[:, th * 512:(th + 1) * 512],
                                    start=(dc == 0), stop=(dc == DC - 1))
                b.inc(ins, "mm")
            if sub == 1:
                return
            b.w(ve, "mm")
            b.inc(ve.tensor_scalar_mul(rowC[0:1, :], psq2[0:1, :], 1.0 / D),
                  "v")
            b.w(sc, "v")
            b.inc(sc.activation(rowC[0:1, :], rowC[0:1, :], ACT.Sqrt,
                                bias=ceps[0:1, :]), "sc")
            b.w(ve, "sc")
            b.inc(ve.reciprocal(rowC[0:1, :], rowC[0:1, :]), "v")
            b.bar()
            with (
                nc.psum_tensor(f"psT_{tag}", [128, 512], F32) as psT,
                nc.psum_tensor(f"psR_{tag}", [1, TPC], F32) as psR,
            ):
                rowify(b, acc, rowD, psT, psR)
            b.bar()
            ve.tensor_mul(rowD[0:1, :], rowD[0:1, :], rowC[0:1, :])
            ve.tensor_scalar_max(rowD[0:1, :], rowD[0:1, :], 1e-5)
            ve.tensor_scalar_mul(rowA[0:1, :], rowD[0:1, :], 1.0 / 127.0)
            ve.tensor_scalar_mul(rowA[0:1, :], rowA[0:1, :], srow[0:1, 4:5])
            ve.reciprocal(rowD[0:1, :], rowD[0:1, :])
            ve.tensor_mul(rowD[0:1, :], rowD[0:1, :], rowC[0:1, :])
            ve.tensor_scalar_mul(rowD[0:1, :], rowD[0:1, :], 127.0)
            ve.tensor_scalar(rowC[0:1, :], rowB[0:1, :], 0.5, 1.0, ALU.mult,
                             ALU.add)
            ve.reciprocal(rowC[0:1, :], rowC[0:1, :])
            ve.tensor_mul(rowC[0:1, :], rowC[0:1, :], rowB[0:1, :])
            b.inc(ve.tensor_mul(rowD[0:1, :], rowD[0:1, :], rowC[0:1, :]), "v")
            with nc.psum_tensor(f"psb2_{tag}", [128, 512], F32) as psb:
                bcast_row(b, rowD[0:1, :], fbt, psb)
                bcast_row(b, rowA[0:1, :], cbt, psb)
            b.bar()
            if sub == 21:
                b.w(ve, "ev")
                ve.tensor_copy(xT[:, 0:TPC], fbt[:, :])
                ve.tensor_copy(xT[:, TPC:2 * TPC], cbt[:, :])
                ve.tensor_copy(xT[:, 2 * TPC:3 * TPC], mbt[:, :])
                b.inc(ve.tensor_copy(xT[:, 3 * TPC:4 * TPC], rbt[:, :]), "ev")
                return
            for dc in range(DC):
                xdc = xT[:, dc * TPC:(dc + 1) * TPC]
                tmp = ln0t[:, (dc % 2) * TPC:(dc % 2 + 1) * TPC]
                ve.tensor_sub(tmp[:, :], xdc, mbt[:, :])
                ve.tensor_scalar_mul(tmp[:, :], tmp[:, :], nw_pp[:, dc:dc + 1])
                ve.tensor_mul(tmp[:, :], tmp[:, :], fbt[:, :])
                ve.tensor_scalar(tmp[:, :], tmp[:, :], MAGIC, MAGIC, ALU.add,
                                 ALU.subtract)
                b.inc(ve.tensor_scalar(k1[:, dc * TPC:(dc + 1) * TPC],
                                       tmp[:, :], 127.0, -127.0,
                                       ALU.min, ALU.max), "v")
            if sub == 2:
                for dc in range(DC):
                    b.inc(ve.tensor_copy(xT[:, dc * TPC:(dc + 1) * TPC],
                                         k1[:, dc * TPC:(dc + 1) * TPC]), "ev")
                return

        b.bar()
        with (
            nc.sbuf_tensor(f"vs_{tag}", [128, 2 * DC * 256], BF16) as vslab,
            nc.sbuf_tensor(f"gs_{tag}", [128, 2 * DC * 256], BF16) as gslab,
            nc.sbuf_tensor(f"sg_{tag}", [128, 2 * TPC], F32) as sgt,
            nc.sbuf_tensor(f"ut_{tag}", [128, 2 * TPC], F32) as ut,
            nc.sbuf_tensor(f"acc2_{tag}", [128, TPC], F32) as acc2,
        ):
            b.w(sy, "cc", 3)
            ve.memset(acc2[:, :], 0.0)
            with (
                nc.psum_tensor(f"pv_{tag}", [128, 2 * TPC], F32) as pv,
                nc.psum_tensor(f"pg_{tag}", [128, 2 * TPC], F32) as pg,
            ):
                base_slab = b.cnt["slab"]
                base_free = b.cnt["free"]
                base_ps = b.cnt["ps"]
                base_ev = b.cnt["ev"]
                base_ug = [b.cnt["ugA"], b.cnt["ugB"]]
                base_sc = b.cnt["sc"]
                for w_ in range(OC):
                    if b.sub == 35 and w_ >= 2:
                        break
                    sl, pp = w_ // 2, w_ % 2
                    if w_ % 2 == 0:
                        vdst = vslab[:, (sl % 2) * DC * 256:
                                     (sl % 2 + 1) * DC * 256]
                        gdst = gslab[:, (sl % 2) * DC * 256:
                                     (sl % 2 + 1) * DC * 256]
                        if sl >= 2:
                            b.w(sy, "ev", base_ev + 2 * sl - 2)
                        ov, og = sl * 256, HID + sl * 256
                        sv, cv = ov // 2048, ov % 2048
                        sg_, cg = og // 2048, og % 2048
                        b.inc(sy.dma_start(
                            out=v3(vdst[:, :], 16, 256),
                            in_=ag_wgv[sv * 2048:(sv + 1) * 2048,
                                       cv:cv + 256].rearrange(
                                "(dc p) o -> p dc o", p=128)), "slab", 16)
                        b.inc(sy.dma_start(
                            out=v3(gdst[:, :], 16, 256),
                            in_=ag_wgv[sg_ * 2048:(sg_ + 1) * 2048,
                                       cg:cg + 256].rearrange(
                                "(dc p) o -> p dc o", p=128)), "slab", 16)
                    vsl = vslab[:, (sl % 2) * DC * 256:(sl % 2 + 1) * DC * 256]
                    gsl = gslab[:, (sl % 2) * DC * 256:(sl % 2 + 1) * DC * 256]
                    if b.sub == 35 and w_ == 0:
                        gp.wait_ge(b.sem["slab"], b.cnt["slab"])
                        for j in range(2):
                            b.inc(gp.dma_start(
                                out=b.y_dbg[j * 128:(j + 1) * 128, :],
                                in_=vslab[:, j * 2048:(j + 1) * 2048]), "stA", 16)
                            b.inc(gp.dma_start(
                                out=b.y_dbg[256 + j * 128:256 + (j + 1) * 128, :],
                                in_=gslab[:, j * 2048:(j + 1) * 2048]), "stA", 16)
                            b.inc(gp.dma_start(
                                out=b.y_dbg[512 + j * 128:512 + (j + 1) * 128, :],
                                in_=vslab[:, 4096 + j * 2048:4096 + (j + 1) * 2048]), "stA", 16)
                            b.inc(gp.dma_start(
                                out=b.y_dbg[768 + j * 128:768 + (j + 1) * 128, :],
                                in_=gslab[:, 4096 + j * 2048:4096 + (j + 1) * 2048]), "stA", 16)
                    te.wait_ge(b.sem["slab"], base_slab + 32 * (sl + 1))
                    if w_ >= 2:
                        te.wait_ge(b.sem["ev"], base_ev + w_ - 1)
                    pvw = pv[:, pp * TPC:(pp + 1) * TPC]
                    pgw = pg[:, pp * TPC:(pp + 1) * TPC]
                    off = (w_ % 2) * 128
                    for th in range(2):
                        for dc in range(DC):
                            te.matmul(pvw[:, th * 512:(th + 1) * 512],
                                      vsl[:, dc * 256 + off:dc * 256 + off + 128],
                                      k1[:, dc * TPC + th * 512:
                                         dc * TPC + (th + 1) * 512],
                                      start=(dc == 0), stop=(dc == DC - 1))
                    for th in range(2):
                        for dc in range(DC):
                            ins = te.matmul(
                                pgw[:, th * 512:(th + 1) * 512],
                                gsl[:, dc * 256 + off:dc * 256 + off + 128],
                                k1[:, dc * TPC + th * 512:
                                   dc * TPC + (th + 1) * 512],
                                start=(dc == 0), stop=(dc == DC - 1))
                    b.inc(ins, "ps")
                    if w_ % 2 == 1:
                        b.inc(te.nop(), "free")
                    sg = sgt[:, pp * TPC:(pp + 1) * TPC]
                    uu = ut[:, pp * TPC:(pp + 1) * TPC]
                    ve.wait_ge(b.sem["ps"], base_ps + w_ + 1)
                    if w_ >= 2:
                        ve.wait_ge(b.sem[["ugA", "ugB"][w_ % 2]],
                                   base_ug[w_ % 2] + 16 * (w_ // 2))
                    b.inc(ve.tensor_mul(sg[:, :], pgw[:, :], cbt[:, :]), "aux")
                    sc.wait_ge(b.sem["aux"], b.cnt["aux"])
                    b.inc(sc.activation(sg[:, :], sg[:, :], ACT.Silu), "sc")
                    ve.wait_ge(b.sem["sc"], base_sc + w_ + 1)
                    if b.sub == 32:
                        ve.tensor_scalar_mul(uu[:, :], pvw[:, :], 1.0)
                    elif b.sub == 33:
                        ve.tensor_scalar_mul(uu[:, :], pgw[:, :], 1.0)
                    else:
                        ve.tensor_mul(uu[:, :], pvw[:, :], sg[:, :])
                    ve.tensor_max(acc2[:, :], acc2[:, :], uu[:, :])
                    ve.scalar_tensor_tensor(acc2[:, :], uu[:, :], -1.0,
                                            acc2[:, :], ALU.mult, ALU.max)
                    b.inc(ve.engine_nop(), "ev")
                    gp.wait_ge(b.sem["ev"], base_ev + w_ + 1)
                    b.inc(gp.dma_start(out=u_hbm[w_, :, :], in_=uu[:, :]),
                          ["ugA", "ugB"][w_ % 2], 16)
            if getattr(b, "sub", 0) == 35:
                b.w(gp, "stA")
                return
            if getattr(b, "sub", 0) in (31, 32, 33):
                b.w(gp, "ugA")
                b.w(gp, "ugB")
                b.inc(gp.dma_start(
                    out=b.y_dbg.rearrange("a (c d) -> (a c) d", c=2),
                    in_=u_hbm.rearrange("oc p t -> (oc p) t")[0:2048, :]),
                    "stA", 16)
                b.w(gp, "stA")
                return
            b.bar()
            with (
                nc.psum_tensor(f"psT2_{tag}", [128, 512], F32) as psT,
                nc.psum_tensor(f"psR2_{tag}", [1, TPC], F32) as psR,
            ):
                rowify(b, acc2, rowD, psT, psR)
            b.bar()
            ve.tensor_mul(rowD[0:1, :], rowD[0:1, :], rowA[0:1, :])
            ve.tensor_scalar_max(rowD[0:1, :], rowD[0:1, :], 1e-5)
            ve.tensor_scalar_mul(rowB[0:1, :], rowD[0:1, :], 0.5 / 127.0)
            b.inc(ve.tensor_scalar_mul(rowB[0:1, :], rowB[0:1, :],
                                       srow[0:1, 5:6]), "v")
            ve.reciprocal(rowD[0:1, :], rowD[0:1, :])
            ve.tensor_mul(rowD[0:1, :], rowD[0:1, :], rowA[0:1, :])
            b.inc(ve.tensor_scalar_mul(rowD[0:1, :], rowD[0:1, :], 127.0), "v")
            with nc.psum_tensor(f"psb3_{tag}", [128, 512], F32) as psb:
                bcast_row(b, rowD[0:1, :], fbt, psb)
                bcast_row(b, rowB[0:1, :], cbt, psb)
            if sub == 3:
                b.w(ve, "ev")
                for dc in range(DC):
                    b.inc(ve.tensor_copy(xT[:, dc * TPC:(dc + 1) * TPC],
                                         fbt[:, :] if dc == 0 else
                                         (cbt[:, :] if dc == 1 else
                                          acc2[:, :])), "ev")
                return
        with (
            nc.sbuf_tensor(f"uq_{tag}", [128, 2 * TPC], F32) as uq,
            nc.sbuf_tensor(f"k2t_{tag}", [128, 2 * TPC], BF16) as k2t,
        ):
            b.w(sy, "ugA")
            b.w(sy, "ugB")
            base_x1 = b.cnt["x1"]
            base_k2 = b.cnt["k2s"]
            base_kpp = [b.cnt["k2A"], b.cnt["k2B"]]
            for oc in range(OC):
                buf = uq[:, (oc % 2) * TPC:(oc % 2 + 1) * TPC]
                kb = k2t[:, (oc % 2) * TPC:(oc % 2 + 1) * TPC]
                if oc >= 2:
                    sy.wait_ge(b.sem["k2s"], base_k2 + oc - 1)
                b.inc(sy.dma_start(out=buf[:, :], in_=u_hbm[oc, :, :]),
                      "x1", 16)
                ve.wait_ge(b.sem["x1"], base_x1 + 16 * (oc + 1))
                if oc >= 2:
                    ve.wait_ge(b.sem[["k2A", "k2B"][oc % 2]],
                               base_kpp[oc % 2] + 16 * (oc // 2))
                ve.tensor_mul(buf[:, :], buf[:, :], fbt[:, :])
                ve.tensor_scalar(buf[:, :], buf[:, :], MAGIC, MAGIC, ALU.add,
                                 ALU.subtract)
                b.inc(ve.tensor_scalar(kb[:, :], buf[:, :], 127.0, -127.0,
                                       ALU.min, ALU.max), "k2s")
                gp.wait_ge(b.sem["k2s"], base_k2 + oc + 1)
                b.inc(gp.dma_start(out=k2_hbm[oc, :, :], in_=kb[:, :]),
                      ["k2A", "k2B"][oc % 2], 16)
        if sub == 41:
            b.w(sy, "k2A")
            b.w(sy, "k2B")
            with nc.sbuf_tensor(f"k2d_{tag}", [128, 2048], BF16) as k2d:
                for blk in range(8):
                    b.w(sy, "ev")
                    b.inc(sy.dma_start(
                        out=v3(k2d[:, :], 4, 512),
                        in_=k2_hbm.rearrange("oc p t -> p oc t")[
                            :, blk * 4:(blk + 1) * 4, 0:512]), "x1", 16)
                    b.w(ve, "x1")
                    b.inc(ve.tensor_copy(
                        xT[:, blk * 2048:(blk + 1) * 2048], k2d[:, :]), "ev")
            return
        b.bar()
        with (
            nc.sbuf_tensor(f"k2h_{tag}", [128, 16 * TPC], BF16) as k2h,
            nc.sbuf_tensor(f"m2s_{tag}", [128, 2 * 16 * 256], BF16) as m2s,
            nc.sbuf_tensor(f"m2e_{tag}", [128, TPC], F32) as m2e,
            nc.psum_tensor(f"pm2_{tag}", [128, 4 * TPC], F32) as pm2,
        ):
            nslab = 0
            base_slab = b.cnt["slab"]
            base_free = b.cnt["free"]
            base_ps = b.cnt["ps"]
            base_ev = b.cnt["ev"]
            base_x2 = b.cnt["x2"]
            for oq in range(4):
                b.w(sy, "k2A")
                b.w(sy, "k2B")
                if oq >= 1:
                    b.w(sy, "ev", base_ev + 8 * oq)
                b.inc(sy.dma_start(
                    out=v3(k2h[:, :], 16, TPC),
                    in_=k2_hbm.rearrange("(oqq oc) p t -> p oqq oc t", oqq=4)[
                        :, oq, :, :]), "x2", 16)
                for slot in range(8):
                    sb_ = m2s[:, (nslab % 2) * 16 * 256:
                              (nslab % 2 + 1) * 16 * 256]
                    if nslab >= 2:
                        b.w(sy, "ev", base_ev + nslab - 1)
                    b.inc(sy.dma_start(
                        out=v3(sb_[:, :], 16, 256),
                        in_=ag_wout.rearrange("(s oc p) m -> p s oc m", s=8,
                                              oc=64)[
                            :, slot, oq * 16:(oq + 1) * 16, :]), "slab", 16)
                    te.wait_ge(b.sem["slab"], b.cnt["slab"])
                    te.wait_ge(b.sem["x2"], base_x2 + 16 * (oq + 1))
                    if nslab >= 2:
                        te.wait_ge(b.sem["ev"], base_ev + nslab - 1)
                    ppw = nslab % 2
                    for dml in range(2):
                        pmw = pm2[:, (ppw * 2 + dml) * TPC:
                                  (ppw * 2 + dml + 1) * TPC]
                        for th in range(2):
                            for ocl in range(16):
                                ins = te.matmul(
                                    pmw[:, th * 512:(th + 1) * 512],
                                    sb_[:, ocl * 256 + dml * 128:
                                        ocl * 256 + dml * 128 + 128],
                                    k2h[:, ocl * TPC + th * 512:
                                        ocl * TPC + (th + 1) * 512],
                                    start=(ocl == 0), stop=(ocl == 15))
                    ins = te.matmul(pm2[:, (ppw * 2 + 1) * TPC + 512:
                                        (ppw * 2 + 2) * TPC],
                                    b.ten["zerot"][:, :], k2h[:, 0:512],
                                    start=False, stop=True)
                    b.inc(ins, "ps")
                    b.inc(te.nop(), "free")
                    ve.wait_ge(b.sem["ps"], base_ps + nslab + 1)
                    if b.sub == 42 and nslab == b.sub42_slot:
                        ppx = b.sub42_slot % 2
                        ve.tensor_copy(xT[:, 0:TPC],
                                       pm2[:, (ppx * 2 + 1) * TPC:
                                           (ppx * 2 + 2) * TPC])
                        b.inc(ve.tensor_copy(xT[:, TPC:2 * TPC],
                                             pm2[:, ppx * 2 * TPC:
                                                 (ppx * 2 + 1) * TPC]), "ev")
                    for dml in range(2):
                        dmc = slot * 2 + dml
                        pmw = pm2[:, (ppw * 2 + dml) * TPC:
                                  (ppw * 2 + dml + 1) * TPC]
                        ve.tensor_mul(m2e[:, :], pmw[:, :], cbt[:, :])
                        ins = ve.tensor_add(
                            xT[:, dmc * TPC:(dmc + 1) * TPC],
                            xT[:, dmc * TPC:(dmc + 1) * TPC], m2e[:, :])
                    b.inc(ins, "ev")
                    nslab += 1
                    if b.sub == 42 and nslab > b.sub42_slot:
                        break
                if b.sub == 42 and nslab > b.sub42_slot:
                    break
            b.w(te, "ev")

def emit_ln_stage(b, xT):
    nc = b.nc
    sy, gp, ve = nc.sync, nc.gpsimd, nc.vector
    t_ = b.ten
    rowA, rowB, mbt, rbt, fbt = (t_["rowA"], t_["rowB"], t_["mbt"], t_["rbt"],
                                 t_["fbt"])
    ag_ln_in, ag_ln, xsp = b.hbm["ag_ln_in"], b.hbm["ag_ln"], b.hbm["xsp"]
    b.bar()
    emit_stats(b, xT, rowA, rowB, "ln1")
    b.bar()
    with nc.psum_tensor("psb_ln", [128, 512], F32) as psb:
        bcast_row(b, rowA[0:1, :], mbt, psb)
        bcast_row(b, rowB[0:1, :], rbt, psb)
    b.bar()
    with nc.sbuf_tensor("lnb", [128, DC * TPC], BF16) as lnb:
        for dc in range(DC):
            xdc = xT[:, dc * TPC:(dc + 1) * TPC]
            ve.tensor_sub(fbt[:, :], xdc, mbt[:, :])
            b.inc(ve.tensor_mul(lnb[:, dc * TPC:(dc + 1) * TPC], fbt[:, :],
                                rbt[:, :]), "v")
        b.w(gp, "v")
        b.inc(gp.dma_start(
            out=ag_ln_in.rearrange("(dc p) t -> p dc t", p=128),
            in_=v3(lnb[:, :], 16, TPC)), "g", 16)
        b.inc(gp.dma_start(out=xsp[:, :], in_=xT[:, :]), "g", 16)
        b.w(gp, "g")
        b.inc(gp.collective_compute("AllGather", ALU.bypass,
                                    ins=[ag_ln_in[:, :]], outs=[ag_ln[:, :]],
                                    replica_groups=[list(range(8))]), "cc")
        b.w(gp, "cc")


def _attn_qkv_tb(b, C, tb):
    import os
    nc = b.nc
    sy, gp, te, ve, sc = nc.sync, nc.gpsimd, nc.tensor, nc.vector, nc.scalar
    rank, off = tb // 2, (tb % 2) * 512
    lsl = C.lns[:, (tb % 2) * DC * 512:(tb % 2 + 1) * DC * 512]
    if tb >= 2:
        b.w(sy, "ev", C.base_ev + 4 * (tb - 1))
        b.w(sy, "sc", C.base_sc + 2 * (tb - 1))
    b.inc(sy.dma_start(
        out=v3(lsl[:, :], 16, 512),
        in_=C.ag_ln.rearrange("(r dc p) t -> p r dc t", r=8, dc=16)[
            :, rank, :, off:off + 512]), "slab", 16)
    te.wait_ge(b.sem["slab"], C.base_slab + 16 * (tb + 1))
    te.wait_ge(b.sem["g"], b.cnt["g"])
    if os.environ.get("ATTN_SLABONLY"):
        b.inc(te.nop(), "free")
        return
    for wsb, dst, bias, scale in ((C.wqb, C.Qh, C.bqs_pp, RSQ),
                                  (C.wkb, C.Kh, C.bk_pp, 1.0)):
        b.w(te, "sc")
        for h2 in range(2):
            for dc in range(DC):
                ins = te.matmul(
                    C.pq[:, h2 * 512:(h2 + 1) * 512],
                    wsb[:, dc * 256 + h2 * 128:dc * 256 + h2 * 128 + 128],
                    lsl[:, dc * 512:(dc + 1) * 512],
                    start=(dc == 0), stop=(dc == DC - 1))
        b.inc(ins, "ps")
        sc.wait_ge(b.sem["ps"], b.cnt["ps"])
        for h2 in range(2):
            ins = sc.activation(
                dst[:, h2 * T + tb * 512:h2 * T + (tb + 1) * 512],
                C.pq[:, h2 * 512:(h2 + 1) * 512], ACT.Identity,
                bias=bias[:, h2:h2 + 1], scale=scale)
        b.inc(ins, "sc")
    if os.environ.get("ATTN_NOV"):
        b.inc(te.nop(), "free")
        return
    for tt in range(4):
        te.wait_ge(b.sem["ev"], max(0, b.cnt["ev"] - 1))
        for dc in range(DC):
            ins = te.matmul(
                C.pvv[:, (tt % 2) * 512:(tt % 2) * 512 + 256],
                lsl[:, dc * 512 + tt * 128:dc * 512 + tt * 128 + 128],
                C.wvb[:, dc * 256:(dc + 1) * 256],
                start=(dc == 0), stop=(dc == DC - 1))
        b.inc(ins, "ps")
        ve.wait_ge(b.sem["ps"], b.cnt["ps"])
        Tc = tb * 4 + tt
        b.inc(ve.tensor_add(C.Vh[:, Tc * 256:(Tc + 1) * 256],
                            C.pvv[:, (tt % 2) * 512:(tt % 2) * 512 + 256],
                            C.bvb[:, :]), "ev")
    b.inc(te.nop(), "free")
    if os.environ.get("ATTN_NOSTATS"):
        return
    b.w(te, "sc")
    spart = int(os.environ.get("ATTN_STATSPART", "0"))
    for h2 in range(2):
        qsl = C.Qh[:, h2 * T + tb * 512:h2 * T + (tb + 1) * 512]
        ksl = C.Kh[:, h2 * T + tb * 512:h2 * T + (tb + 1) * 512]
        if spart != 2:
            b.w(te, "v")
            b.inc(te.matmul(C.pr1[0:1, :], C.bk_cb[:, h2:h2 + 1], qsl,
                            start=True, stop=True), "mm")
            b.inc(te.matmul(C.pr2[0:1, :], C.bqs_cb[:, h2:h2 + 1], ksl,
                            start=True, stop=True), "mm")
            b.w(ve, "mm")
            ve.tensor_reduce(C.st(h2, 0)[0:1, tb:tb + 1], C.pr1[0:1, :],
                             AX, ALU.max)
            b.inc(ve.tensor_reduce(C.st(h2, 2)[0:1, tb:tb + 1],
                                   C.pr2[0:1, :], AX, ALU.max), "v")
        if spart == 1:
            continue
        if spart == 2:
            b.w(te, "v")
        for sl_, bias_, j_ in ((qsl, C.bqs_pp, 1), (ksl, C.bk_pp, 3)):
            ve.tensor_scalar(C.atmp[:, :], sl_, bias_[:, h2:h2 + 1], None,
                             ALU.subtract)
            b.inc(ve.tensor_mul(C.atmp[:, :], C.atmp[:, :], C.atmp[:, :]),
                  "aux")
            te.wait_ge(b.sem["aux"], b.cnt["aux"])
            b.inc(te.matmul(C.pr1[0:1, :], C.ocf[:, :], C.atmp[:, :],
                            start=True, stop=True), "mm")
            b.w(ve, "mm")
            b.inc(ve.tensor_reduce(C.st(h2, j_)[0:1, tb:tb + 1],
                                   C.pr1[0:1, :], AX, ALU.max), "v")


def _attn_sweep_qi(b, C, h2, qi):
    nc = b.nc
    te, ve, sc = nc.tensor, nc.vector, nc.scalar
    nmh = C.nmhall[:, h2 * 16 + qi:h2 * 16 + qi + 1]
    base_ps = b.cnt["ps"]
    base_sc2 = b.cnt["sc"]
    base_ev = b.cnt["ev"]
    sc.wait_ge(b.sem["v"], b.cnt["v"])
    ngrp = 0
    for kvb in range(qi + 1):
        for grp in range(2):
            ppg = ngrp % 2
            plw = C.pl[:, ppg * 1024:(ppg + 1) * 1024]
            pbw = C.pbuf[:, ppg * 1024:(ppg + 1) * 1024]
            if ngrp >= 2:
                te.wait_ge(b.sem["sc"], base_sc2 + ngrp - 1)
            for s2_ in range(2):
                s_ = grp * 2 + s2_
                ins = te.matmul(
                    plw[:, s2_ * 512:(s2_ + 1) * 512],
                    C.Kh[:, h2 * T + kvb * 512 + s_ * 128:
                         h2 * T + kvb * 512 + (s_ + 1) * 128],
                    C.Qh[:, h2 * T + qi * 512:h2 * T + (qi + 1) * 512],
                    start=True, stop=True)
            b.inc(ins, "ps")
            if kvb == qi:
                ve.wait_ge(b.sem["ps"], base_ps + ngrp + 1)
                for s2_ in range(2):
                    s_ = grp * 2 + s2_
                    ins = ve.tensor_add(
                        plw[:, s2_ * 512:(s2_ + 1) * 512],
                        plw[:, s2_ * 512:(s2_ + 1) * 512],
                        C.dmask[:, s_ * 512:(s_ + 1) * 512])
                b.inc(ins, "aux")
                sc.wait_ge(b.sem["aux"], b.cnt["aux"])
            else:
                sc.wait_ge(b.sem["ps"], base_ps + ngrp + 1)
            if ngrp >= 2:
                sc.wait_ge(b.sem["ev"], base_ev + ngrp - 1)
            b.inc(sc.activation(pbw[:, :], plw[:, :], ACT.Exp, bias=nmh),
                  "sc")
            te.wait_ge(b.sem["sc"], base_sc2 + ngrp + 1)
            first = (ngrp == 0)
            last = (kvb == qi and grp == 1)
            for s2_ in range(2):
                s_ = grp * 2 + s2_
                Tc = kvb * 4 + s_
                te.matmul(C.pO[:, :],
                          C.Vh[:, Tc * 256 + h2 * 128:Tc * 256 +
                               (h2 + 1) * 128],
                          pbw[:, s2_ * 512:(s2_ + 1) * 512],
                          start=(first and s2_ == 0),
                          stop=(last and s2_ == 1))
                ins = te.matmul(C.pd[0:1, :], C.ocb[:, :],
                                pbw[:, s2_ * 512:(s2_ + 1) * 512],
                                start=(first and s2_ == 0),
                                stop=(last and s2_ == 1))
            b.inc(ins, "ev")
            ngrp += 1
    b.w(ve, "ev")
    ve.tensor_copy(C.rowA[0:1, 0:512], C.pd[0:1, :])
    b.inc(ve.reciprocal(C.rowA[0:1, 0:512], C.rowA[0:1, 0:512]), "v")
    b.w(te, "v")
    b.inc(te.matmul(C.pl[:, 0:512], C.orf[0:1, :], C.rowA[0:1, 0:512],
                    start=True, stop=True), "mm")
    b.w(ve, "mm")
    ve.tensor_copy(C.dinv[:, :], C.pl[:, 0:512])
    b.inc(ve.tensor_mul(
        C.Ofin[:, h2 * T + qi * 512:h2 * T + (qi + 1) * 512],
        C.pO[:, :], C.dinv[:, :]), "ev")
    b.w(te, "ev")
    b.w(sc, "ev")


def emit_attention(b, wq_c, wk_c, wv_c, sub=0):
    from types import SimpleNamespace
    nc = b.nc
    sy, gp, te, ve, sc = nc.sync, nc.gpsimd, nc.tensor, nc.vector, nc.scalar
    t_ = b.ten
    ag_ln, ag_o_in, ag_o = b.hbm["ag_ln"], b.hbm["ag_o_in"], b.hbm["ag_o"]
    C = SimpleNamespace(ag_ln=ag_ln, bqs_pp=t_["bqs_pp"], bk_pp=t_["bk_pp"],
                        bqs_cb=t_["bqs_cb"], bk_cb=t_["bk_cb"],
                        dmask=t_["dmask"], orf=t_["orf"], ocf=t_["ocf"],
                        ocb=t_["ocb"], rowA=t_["rowA"], bvb=t_["bvb"])
    b.bar()
    with (
        nc.sbuf_tensor("Qh", [128, 2 * T], BF16) as Qh,
        nc.sbuf_tensor("Kh", [128, 2 * T], BF16) as Kh,
        nc.sbuf_tensor("Vh", [128, 64 * 256], BF16) as Vh,
        nc.sbuf_tensor("bstat", [1, 8 * 16], F32) as bstat,
        nc.sbuf_tensor("bqbk", [1, 32], F32) as bqbk,
        nc.sbuf_tensor("bkrep", [128, 32], BF16) as bkrep,
        nc.sbuf_tensor("otile", [128, 16], F32) as otile,
        nc.sbuf_tensor("nmhall", [128, 2 * 16], F32) as nmhall,
        nc.sbuf_tensor("mrowb", [1, 6 * 16], F32) as mrowb,
        nc.sbuf_tensor("dinv", [128, 512], F32) as dinv,
    ):
        C.Qh, C.Kh, C.Vh, C.nmhall, C.dinv = Qh, Kh, Vh, nmhall, dinv
        C.st = lambda h2, j: bstat[0:1, (h2 * 4 + j) * 16:
                                   (h2 * 4 + j) * 16 + 16]
        with (
            nc.sbuf_tensor("wqb", [128, DC * 256], BF16) as wqb,
            nc.sbuf_tensor("wkb", [128, DC * 256], BF16) as wkb,
            nc.sbuf_tensor("wvb", [128, DC * 256], BF16) as wvb,
            nc.sbuf_tensor("lns", [128, 2 * DC * 512], BF16) as lns,
            nc.sbuf_tensor("atmp", [128, 512], F32) as atmp,
            nc.psum_tensor("pq", [128, 2 * 512], F32) as pq,
            nc.psum_tensor("pvv", [128, 2 * 512], F32) as pvv,
            nc.psum_tensor("pr1", [1, 512], F32) as pr1,
            nc.psum_tensor("pr2", [1, 512], F32) as pr2,
            nc.psum_tensor("pbc", [128, 256], F32) as pbc,
        ):
            C.wqb, C.wkb, C.wvb, C.lns, C.atmp = wqb, wkb, wvb, lns, atmp
            C.pq, C.pvv, C.pr1, C.pr2 = pq, pvv, pr1, pr2
            b.w(te, "ld")
            b.inc(te.matmul(pbc[:, 0:256], C.orf[0:1, :],
                            t_["bv_row"][0:1, 0:256], start=True, stop=True),
                  "mm")
            b.w(ve, "mm")
            b.inc(ve.tensor_copy(C.bvb[:, :], pbc[:, :]), "v")
            b.inc(ve.memset(otile[:, :], 1.0), "v")
            ve.wait_ge(b.sem["v"], b.cnt["v"])
            for h2 in range(2):
                b.inc(ve.tensor_scalar_mul(bkrep[:, h2 * 16:(h2 + 1) * 16],
                                           otile[:, :],
                                           C.bk_pp[:, h2:h2 + 1]), "v")
            ve.wait_ge(b.sem["v"], b.cnt["v"])
            b.w(te, "v")
            for h2 in range(2):
                ins = te.matmul(pr1[0:1, h2 * 16:(h2 + 1) * 16],
                                C.bqs_cb[:, h2:h2 + 1],
                                bkrep[:, h2 * 16:(h2 + 1) * 16],
                                start=True, stop=True)
            b.inc(ins, "mm")
            b.w(ve, "mm")
            b.inc(ve.tensor_copy(bqbk[0:1, :], pr1[0:1, 0:32]), "v")
            for wsb, wsrc in ((wqb, wq_c), (wkb, wk_c), (wvb, wv_c)):
                b.inc(gp.dma_start(
                    out=v3(wsb[:, :], 16, 256),
                    in_=wsrc.rearrange("(dc p) m -> p dc m", p=128)),
                    "g", 16)
            b.w(sy, "cc")
            import os
            C.base_slab = b.cnt["slab"]
            C.base_ev = b.cnt["ev"]
            C.base_sc = b.cnt["sc"]
            for tb in range(int(os.environ.get("ATTN_TBMAX", "16"))):
                _attn_qkv_tb(b, C, tb)
        b.bar()
        b.w(sc, "v")
        for h2 in range(2):
            sc.activation(C.st(h2, 1)[0:1, :], C.st(h2, 1)[0:1, :], ACT.Sqrt)
            b.inc(sc.activation(C.st(h2, 3)[0:1, :], C.st(h2, 3)[0:1, :],
                                ACT.Sqrt), "sc")
        if sub == 51:
            return
        b.bar()
        with (
            nc.sbuf_tensor("Ofin", [128, 2 * T], BF16) as Ofin,
            nc.sbuf_tensor("pbuf", [128, 2 * 1024], BF16) as pbuf,
            nc.psum_tensor("pl", [128, 2 * 1024], F32) as pl,
            nc.psum_tensor("pO", [128, 512], F32) as pO,
            nc.psum_tensor("pd", [1, 512], F32) as pd,
            nc.psum_tensor("pmh", [128, 16], F32) as pmh,
        ):
            C.Ofin, C.pbuf, C.pl, C.pO, C.pd = Ofin, pbuf, pl, pO, pd
            b.w(ve, "sc")
            for h2 in range(2):
                mr = mrowb[0:1, h2 * 16:(h2 + 1) * 16]
                cg = mrowb[0:1, (2 + h2) * 16:(2 + h2) * 16 + 1]
                kg = mrowb[0:1, (4 + h2) * 16:(4 + h2) * 16 + 1]
                def _vf(ins):
                    b.inc(ins, "v")
                    ve.wait_ge(b.sem["v"], b.cnt["v"])
                _vf(ve.tensor_reduce(cg, C.st(h2, 3)[0:1, :], AX, ALU.max))
                _vf(ve.tensor_reduce(kg, C.st(h2, 2)[0:1, :], AX, ALU.max))
                _vf(ve.tensor_single_scalar(mr, C.st(h2, 1)[0:1, :], cg,
                                            ALU.mult))
                _vf(ve.tensor_add(mr, mr, C.st(h2, 0)[0:1, :]))
                _vf(ve.tensor_single_scalar(mr, mr, kg, ALU.add))
                _vf(ve.tensor_single_scalar(mr, mr,
                                            bqbk[0:1, h2 * 16:h2 * 16 + 1],
                                            ALU.subtract))
                b.inc(ve.tensor_scalar_mul(mr, mr, -1.0), "v")
                ve.wait_ge(b.sem["v"], b.cnt["v"])
                b.w(te, "v")
                b.inc(te.matmul(pmh[:, :], C.orf[0:1, :], mr, start=True,
                                stop=True), "mm")
                b.w(ve, "mm")
                b.inc(ve.tensor_copy(nmhall[:, h2 * 16:(h2 + 1) * 16],
                                     pmh[:, :]), "v")
            for h2 in range(2):
                for qi in range(16):
                    _attn_sweep_qi(b, C, h2, qi)
            if sub == 53:
                b.w(gp, "ev")
                for j in range(8):
                    b.inc(gp.dma_start(out=b.y_dbg[j * 128:(j + 1) * 128, :],
                                       in_=Ofin[:, j * 2048:(j + 1) * 2048]),
                          "g", 16)
                b.w(gp, "g")
                return
            if sub == 54:
                b.w(gp, "ev")
                b.inc(gp.dma_start(out=b.y_dbg[200:328, 0:8],
                                   in_=C.bqs_cb[:, :]), "g", 16)
                b.inc(gp.dma_start(out=b.y_dbg[400:528, 0:8],
                                   in_=C.bk_cb[:, :]), "g", 16)
                b.inc(gp.dma_start(out=b.y_dbg[600:728, 0:8],
                                   in_=C.bqs_pp[:, :]), "g", 16)
                b.inc(gp.dma_start(out=b.y_dbg[0:1, 0:128],
                                   in_=bstat[0:1, :]), "g", 16)
                b.inc(gp.dma_start(out=b.y_dbg[1:2, 0:96],
                                   in_=mrowb[0:1, :]), "g", 16)
                b.inc(gp.dma_start(out=b.y_dbg[2:3, 0:32],
                                   in_=bqbk[0:1, :]), "g", 16)
                b.inc(gp.dma_start(out=b.y_dbg[8:136, 0:32],
                                   in_=nmhall[:, :]), "g", 16)
                b.w(gp, "g")
                return
            if sub == 52:
                return
            b.w(gp, "ev")
            for h2 in range(2):
                b.inc(gp.dma_start(out=ag_o_in[h2 * 128:(h2 + 1) * 128, :],
                                   in_=Ofin[:, h2 * T:(h2 + 1) * T]),
                      "g", 16)
            b.w(gp, "g")
            b.inc(gp.collective_compute(
                "AllGather", ALU.bypass, ins=[ag_o_in[:, :]],
                outs=[ag_o[:, :]], replica_groups=[list(range(8))]), "cc")
            b.w(gp, "cc")


def emit_outproj(b, wo_in, xT2):
    nc = b.nc
    sy, gp, te, ve, sc = nc.sync, nc.gpsimd, nc.tensor, nc.vector, nc.scalar
    ag_o, xsp = b.hbm["ag_o"], b.hbm["xsp"]
    bo_pp = b.ten["bo_pp"]
    pid = nc.sync.partition_id()
    tok0 = pid * TPC
    b.bar()
    with (
        nc.sbuf_tensor("oall", [128, 16 * TPC], BF16) as oall,
        nc.sbuf_tensor("wos2", [128, 2 * DC * 128], BF16) as wos2,
        nc.sbuf_tensor("ope", [128, TPC], F32) as ope,
        nc.psum_tensor("po2", [128, 2 * TPC], F32) as po2,
    ):
        b.w(sy, "cc")
        b.w(sy, "g")
        b.inc(sy.dma_start(out=xT2[:, :], in_=xsp[:, :]), "x2", 16)
        for hc in range(16):
            rank, loc = hc // 2, hc % 2
            b.inc(sy.dma_start(
                out=oall[:, hc * TPC:(hc + 1) * TPC],
                in_=ag_o[rank * 256 + loc * 128:rank * 256 + (loc + 1) * 128,
                         bass.ds(tok0, TPC)]), "x1", 16)
        base_x1 = b.cnt["x1"]
        base_g = b.cnt["g"]
        base_ps = b.cnt["ps"]
        base_ev = b.cnt["ev"]
        for dmc in range(DC):
            wsl = wos2[:, (dmc % 2) * DC * 128:(dmc % 2 + 1) * DC * 128]
            if dmc >= 2:
                gp.wait_ge(b.sem["ps"], base_ps + dmc - 1)
            b.inc(gp.dma_start(
                out=v3(wsl[:, :], 16, 128),
                in_=wo_in.rearrange("(hc p) m -> p hc m", p=128)[
                    :, :, dmc * 128:(dmc + 1) * 128]), "g", 16)
            te.wait_ge(b.sem["x1"], base_x1)
            te.wait_ge(b.sem["g"], base_g + 16 * (dmc + 1))
            if dmc >= 2:
                te.wait_ge(b.sem["ev"], base_ev + dmc - 1)
            ppw = dmc % 2
            pmw = po2[:, ppw * TPC:(ppw + 1) * TPC]
            for th in range(2):
                for hc in range(16):
                    ins = te.matmul(
                        pmw[:, th * 512:(th + 1) * 512],
                        wsl[:, hc * 128:(hc + 1) * 128],
                        oall[:, hc * TPC + th * 512:hc * TPC + (th + 1) * 512],
                        start=(hc == 0), stop=(hc == 15))
            b.inc(ins, "ps")
            sc.wait_ge(b.sem["ps"], base_ps + dmc + 1)
            b.inc(sc.activation(ope[:, :], pmw[:, :], ACT.Identity,
                                bias=bo_pp[:, dmc:dmc + 1]), "sc")
            ve.wait_ge(b.sem["sc"], b.cnt["sc"])
            ve.wait_ge(b.sem["x2"], b.cnt["x2"])
            b.inc(ve.tensor_add(xT2[:, dmc * TPC:(dmc + 1) * TPC],
                                xT2[:, dmc * TPC:(dmc + 1) * TPC],
                                ope[:, :]), "ev")
        b.w(te, "ev")


def emit_store_nat(b, xT, y):
    nc = b.nc
    sy, te, ve = nc.sync, nc.tensor, nc.vector
    idf = b.ten["idf"]
    b.bar()
    with (
        nc.sbuf_tensor("ostA", [128, 2048], F32) as ostA,
        nc.sbuf_tensor("ostB", [128, 2048], F32) as ostB,
        nc.psum_tensor("pst", [128, 512], F32) as pt,
    ):
        obufs = [ostA, ostB]
        base_pp = [b.cnt["stA"], b.cnt["stB"]]
        for tt in range(8):
            buf = obufs[tt % 2]
            stp = ["stA", "stB"][tt % 2]
            for grp in range(4):
                b.w(te, "ev")
                if tt >= 2 and grp == 0:
                    te.wait_ge(b.sem[stp], base_pp[tt % 2] + 16 * (tt // 2))
                for j in range(4):
                    dc = grp * 4 + j
                    ins = te.transpose(
                        pt[:, j * 128:(j + 1) * 128],
                        xT[:, dc * TPC + tt * 128:dc * TPC + (tt + 1) * 128],
                        idf[:])
                b.inc(ins, "mm")
                b.w(ve, "mm")
                for j in range(4):
                    dc = grp * 4 + j
                    ins = ve.tensor_copy(buf[:, dc * 128:(dc + 1) * 128],
                                         pt[:, j * 128:(j + 1) * 128])
                b.inc(ins, "ev")
            b.w(sy, "ev")
            b.inc(sy.dma_start(out=y[tt * 128:(tt + 1) * 128, :],
                               in_=buf[:, :]), stp, 16)


from concourse.bass_utils import run_bass_kernel_spmd


def _shard_inputs(inputs):
    x = np.ascontiguousarray(inputs["L0_attn_query"][0], dtype=np.float32)
    w_gv = np.ascontiguousarray(inputs["L0_ffn_w_gv"], dtype=np.float32)
    w_out = np.ascontiguousarray(inputs["L0_ffn_w_out"], dtype=np.float32)
    nw = np.asarray(inputs["L0_ffn_norm_w"], np.float32).reshape(1, 2048)
    wq = np.asarray(inputs["L0_attn_w_q"], np.float32)
    wk = np.asarray(inputs["L0_attn_w_k"], np.float32)
    wv = np.asarray(inputs["L0_attn_w_v"], np.float32)
    wo = np.ascontiguousarray(inputs["L0_attn_w_o"], dtype=np.float32)
    bq = np.asarray(inputs["L0_attn_b_q"], np.float32).reshape(1, 2048)
    bk = np.asarray(inputs["L0_attn_b_k"], np.float32).reshape(1, 2048)
    bv = np.asarray(inputs["L0_attn_b_v"], np.float32).reshape(1, 2048)
    bo = np.asarray(inputs["L0_attn_b_o"], np.float32).reshape(1, 2048)
    maps = []
    for c in range(8):
        maps.append(dict(
            x_c=np.ascontiguousarray(x[c * 1024:(c + 1) * 1024]),
            wgv_c=np.ascontiguousarray(w_gv[c * 2048:(c + 1) * 2048]),
            wout_c=np.ascontiguousarray(w_out[c * 256:(c + 1) * 256]),
            nw_in=nw,
            wq_c=np.ascontiguousarray(wq[:, c * 256:(c + 1) * 256]),
            wk_c=np.ascontiguousarray(wk[:, c * 256:(c + 1) * 256]),
            wv_c=np.ascontiguousarray(wv[:, c * 256:(c + 1) * 256]),
            wo_in=wo,
            bq_c=np.ascontiguousarray(bq[:, c * 256:(c + 1) * 256]),
            bk_c=np.ascontiguousarray(bk[:, c * 256:(c + 1) * 256]),
            bv_c=np.ascontiguousarray(bv[:, c * 256:(c + 1) * 256]),
            bo_in=bo,
        ))
    return maps


def kernel(**inputs):
    nc = build(stage=3, sub=0)
    maps = _shard_inputs(inputs)
    res = run_bass_kernel_spmd(nc, maps, core_ids=list(range(8)))
    out = np.concatenate([res.results[c]["y"] for c in range(8)], axis=0)
    return out.astype(np.float32, copy=False)[None]
